# version 02
# baseline (speedup 1.0000x reference)
import numpy as np
import concourse.bass as bass
import concourse.mybir as mybir

F32 = mybir.dt.float32
BF16 = mybir.dt.bfloat16
AF = mybir.ActivationFunctionType
ALU = mybir.AluOpType
AX = mybir.AxisListType

D = 1024
F = 64
NE = 16
EPS_LN = 1e-5
EPS_RMS = 1e-6
ETA_MIN = 1e-6
FD = float(F * D)
ENGINES = ("sync", "vector", "scalar", "tensor", "gpsimd")


class Op:
    __slots__ = ("eng", "fn", "deps", "grp", "seq", "inc_id", "_grp_snapshot",
                 "nf", "acc")
    def __init__(self, eng, fn, deps, grp, seq, nf=False, acc=False):
        self.eng, self.fn, self.deps, self.grp, self.seq = eng, fn, list(deps), grp, seq
        self.inc_id = None
        self._grp_snapshot = None
        self.nf = nf
        self.acc = acc


class Sched:

    def __init__(self):
        self.ops = []
        self.grp_count = {}

    def add(self, eng, fn, deps=(), grp=None, nf=False, acc=False):
        deps = [d for d in deps if d is not None]
        for d in deps:
            assert d.seq < len(self.ops)
        op = Op(eng, fn, deps, grp, len(self.ops), nf=nf, acc=acc)
        if grp is not None:
            self.grp_count[grp] = self.grp_count.get(grp, 0) + 1
            op.inc_id = self.grp_count[grp]
        self.ops.append(op)
        return op

    @staticmethod
    def _fence_needed(d, op):
        if d.grp is not None:
            return True
        if d.eng != op.eng:
            return True
        if op.grp is not None:
            return True
        if d.eng == "vector":
            return True
        if d.acc:
            return True
        return False

    def emit(self, nc):
        last = {}
        for op in self.ops:
            if op.grp is not None:
                continue
            prev = last.get(op.eng)
            if prev is not None:
                if op.eng == "vector" and not op.nf and prev not in op.deps:
                    op.deps.append(prev)
                elif op.eng == "scalar" and prev.acc and prev not in op.deps:
                    op.deps.append(prev)
            last[op.eng] = op
        needed = set()
        for op in self.ops:
            for d in op.deps:
                if d.grp is None and not self._fence_needed(d, op):
                    continue
                needed.add(d)
        counts = {e: 0 for e in ENGINES}
        for op in self.ops:
            if op.grp is None and op in needed:
                counts[op.eng] += 1
                op.inc_id = counts[op.eng]
        grp_run = {g: 0 for g in self.grp_count}
        for op in self.ops:
            op._grp_snapshot = dict(grp_run) if any(
                d.grp is not None for d in op.deps) else None
            if op.grp is not None:
                grp_run[op.grp] += 1
        sem_objs, ctxs, gsems = {}, [], {}
        for e in ENGINES:
            c = nc.semaphore("sd_" + e)
            ctxs.append(c)
            sem_objs[e] = c.__enter__()
        for g in self.grp_count:
            c = nc.semaphore("gd_" + g)
            ctxs.append(c)
            gsems[g] = c.__enter__()
        blk = nc.Block()
        block = blk.__enter__()

        def make(engname):
            def _(eng):
                waited_e = {e: 0 for e in ENGINES}
                waited_g = {g: 0 for g in self.grp_count}
                for op in self.ops:
                    if op.eng != engname:
                        continue
                    for d in op.deps:
                        if d.grp is not None:
                            vv = 16 * op._grp_snapshot[d.grp]
                            if vv > waited_g[d.grp]:
                                eng.wait_ge(gsems[d.grp], vv)
                                waited_g[d.grp] = vv
                        else:
                            if d.inc_id is None:
                                continue
                            if not self._fence_needed(d, op):
                                continue
                            if d.inc_id > waited_e[d.eng]:
                                eng.wait_ge(sem_objs[d.eng], d.inc_id)
                                waited_e[d.eng] = d.inc_id
                    inst = op.fn(eng)
                    if inst is not None:
                        try:
                            inst.annotate("op%d" % op.seq)
                        except Exception:
                            pass
                    if op.grp is not None:
                        assert inst is not None
                        inst.then_inc(gsems[op.grp], 16)
                    elif op.inc_id is not None:
                        if inst is None:
                            assert engname not in ("vector",), \
                                "vector op returned no instr; fence unsound"
                            eng.sem_inc(sem_objs[engname], 1)
                        else:
                            inst.then_inc(sem_objs[engname], 1)
            return _
        block.sync(make("sync"))
        block.vector(make("vector"))
        block.scalar(make("scalar"))
        block.tensor(make("tensor"))
        block.gpsimd(make("gpsimd"))
        blk.__exit__(None, None, None)
        for c in reversed(ctxs):
            c.__exit__(None, None, None)


class KB:
    def __init__(self, T=4096, C=64, debug=(), with_chain=True, with_post=True,
                 with_ffn=True, with_gate=None):
        self.T, self.C = T, C
        self.NCH = T // C
        self.PB = min(512, T)
        self.NPB = T // self.PB
        self.CPP = self.PB // C
        self.NT = T // 128
        self.debug = set(debug)
        self.with_chain, self.with_post, self.with_ffn = with_chain, with_post, with_ffn
        self.with_gate = with_ffn if with_gate is None else with_gate
        self.QT = min(512, T)
        self.NQ = T // self.QT
        self.sch = Sched()
        self.dbg = {}
        self.last_dma_out = []

    def v(self, fn, deps=(), nf=False):
        return self.sch.add("vector", fn, deps, nf=nf)

    def s(self, fn, deps=(), acc=False):
        return self.sch.add("scalar", fn, deps, acc=acc)

    def t(self, fn, deps=()):
        return self.sch.add("tensor", fn, deps)

    def g(self, fn, deps=()):
        return self.sch.add("gpsimd", fn, deps)

    def dma(self, grp, out, in_, deps=(), eng="sync"):
        return self.sch.add(eng, lambda e: e.dma_start(out=out, in_=in_),
                            deps, grp=grp)

    def dmat(self, grp, out, in_, deps=(), eng="sync"):
        return self.sch.add(eng, lambda e: e.dma_start_transpose(out=out, in_=in_),
                            deps, grp=grp)

    def dbg_add(self, name, shape, src_fn, deps, d=F32):
        if name not in self.debug:
            return
        t = self.nc.dram_tensor("dbg_" + name, shape, d, kind="ExternalOutput").ap()
        self.dbg[name] = (t, src_fn, deps)

    def flush_dbg(self):
        for name, (t, src_fn, deps) in self.dbg.items():
            self.last_dma_out.append(
                self.dma("dbgout", t[:, :] if len(t.shape) == 2 else t[:, :, :],
                         src_fn(), deps=deps))
        self.dbg = {}

    def build(self):
        nc = bass.Bass("TRN2", target_bir_lowering=False, debug=False,
                       detect_race_conditions=False)
        self.nc = nc
        T, C, NT = self.T, self.C, self.NT
        PB = self.PB

        dt = lambda n, s, k="ExternalInput", d=F32: nc.dram_tensor(n, s, d, kind=k).ap()
        self.x_hbm = dt("x", [T, D])
        self.wkT_h = dt("wkT", [D, F])
        self.wqT_h = dt("wqT", [D, F])
        self.wvT_h = dt("wvT", [D, D])
        self.woT_h = dt("woT", [D, D])
        self.nw_h = dt("nw", [128, 8])
        self.eta_h = dt("eta", [F, 1])
        self.ident_h = dt("ident_in", [128, 128])
        self.gwT_h = dt("gwT", [D, NE])
        self.we1T_h = dt("we1T", [NE, D, 512])
        self.we3T_h = dt("we3T", [NE, D, 512])
        self.we2T_h = dt("we2T", [NE, 512, 1023])
        self.w2ti_h = dt("w2ti", [NE, 1, 1023])
        self.be1_h = dt("be1", [NE, 128, 4])
        self.be3_h = dt("be3", [NE, 128, 4])
        self.be2_h = dt("be2", [NE, 1, 1023])
        self.ws1T_h = dt("ws1T", [D, 1024])
        self.ws3T_h = dt("ws3T", [D, 1024])
        self.ws2T_h = dt("ws2T", [1024, 1023])
        self.ws2ti_h = dt("ws2ti", [1, 1023])
        self.bs1_h = dt("bs1", [128, 8])
        self.bs3_h = dt("bs3", [128, 8])
        self.bs2_h = dt("bs2", [1, 1023])
        self.out_hbm = dt("out", [T, D], k="ExternalOutput")
        self.xn_sc = dt("xn_sc", [8, 128, T], k="Internal", d=BF16)
        self.xnr_sc = dt("xnr_sc", [8, 128, T], k="Internal", d=BF16)
        self.s_sc = dt("s_sc", [T, D], k="Internal", d=BF16)
        self.wffn_sc = dt("wffn_sc", [NE, 128, 2 * 4096 + 4 * 1023], k="Internal",
                          d=BF16)
        self.wsh_sc = dt("wsh_sc", [128, 2 * 8192 + 8 * 1023], k="Internal", d=BF16)
        PIECE = 8192
        self.PIECE = PIECE

        sb = lambda n, s, d=F32: nc.alloc_sbuf_tensor(n, s, d).ap()
        psa = lambda n, s, d=F32: nc.alloc_psum_tensor(n, s, d).ap()
        self.sb = sb

        self.wv_sb = sb("wv_sb", [128, 8 * D], BF16)
        self.wk_sb = sb("wk_sb", [128, 8 * F], BF16)
        self.wq_sb = sb("wq_sb", [128, 8 * F], BF16)
        self.gw_sb = sb("gw_sb", [128, 8 * NE], BF16)
        self.w16 = sb("w16", [128, NT * NE])
        self.ident = sb("ident", [128, 128], BF16)
        self.ident_f = sb("ident_f", [128, 128])
        self.ones_r = sb("ones_r", [1, 128])
        self.ones_cf = sb("ones_cf", [128, 1])
        self.ones_cb = sb("ones_cb", [128, 1], BF16)
        self.idcell = sb("idcell", [1, 1])
        self.etac = sb("etac", [F, 1])
        self.wcol = sb("wcol", [128, 8])
        self.xt = sb("xt", [128, D])
        self.xsq = sb("xsq", [128, D])
        self.xnt = sb("xnt", [128, D], BF16)
        self.xrt = sb("xrt", [128, D], BF16)
        self.stt_ = [sb("stt0", [128, 8]), sb("stt1", [128, 8])]
        self.wtmp = sb("wtmp", [128, 1024])
        self.wstg = sb("wstg", [128, 1024], BF16)
        self.stg = sb("stg", [128, 8 * 128], BF16)
        self.xnr_pb = sb("xnr_pb", [128, 8 * PB], BF16)
        self.v_pb = sb("v_pb", [128, 8 * PB], BF16)
        self.k_pb = sb("k_pb", [F, PB])
        self.ke_pb = sb("ke_pb", [F, PB])
        self.kb_pb = sb("kb_pb", [F, PB], BF16)
        self.keb_pb = sb("keb_pb", [F, PB], BF16)
        self.pscr = sb("pscr", [F, PB])
        C_ = C
        self.evac64 = sb("evac64", [C_, 4 * C_], BF16)
        self.a_bf = sb("a_bf", [128, 8 * C_], BF16)
        self.sq_sb = sb("sq_sb", [128, 8 * C_])
        self.sq2_sb = sb("sq2_sb", [128, 8 * C_])
        self.vtok = sb("vtok", [C_, D], BF16)
        self.kph = sb("kph", [F, C_], BF16)
        self.kphT = sb("kphT", [C_, F], BF16)
        self.m_prev = sb("m_prev", [F, D])
        self.m_bf = sb("m_bf", [F, D], BF16)
        self.uebc = sb("uebc", [F, 2])
        self.sq64 = sb("sq64", [F, C_])
        self.htf = sb("htf", [1, C_ * C_], BF16)
        self.gvtf = sb("gvtf", [1, C_ * C_], BF16)
        self.avttf = sb("avttf", [1, C_ * C_], BF16)
        self.gktf = sb("gktf", [1, C_ * C_], BF16)
        self.q6r = sb("q6r", [1, 6 * C_])
        self.bkr = sb("bkr", [1, 2 * C_])
        self.kapr = sb("kapr", [1, C_ + 1])
        self.aamr = sb("aamr", [1, C_])
        self.mhatr = sb("mhatr", [1, C_])
        self.rowsc = sb("rowsc", [1, 8 * C_])
        self.rows2 = sb("rows2", [1, 8 * C_])
        self.phr = sb("phr", [1, C_])
        self.ghr = sb("ghr", [1, C_])
        self.gher = sb("gher", [1, C_])
        self.q1r = sb("q1r", [1, C_])
        self.q2r = sb("q2r", [1, C_])
        self.scr = sb("scr", [1, C_])
        self.scr6 = sb("scr6", [1, 8])
        self.gcol = sb("gcol", [C_, 1], BF16)
        self.st = sb("st", [1, 32])
        QT = self.QT
        self.xnsl = sb("xnsl", [128, 8 * QT], BF16)
        self.y_q = sb("y_q", [128, (QT // 128) * D])
        self.wff = sb("wff", [128, 2 * PIECE], BF16)
        self.h_sb = sb("h_sb", [128, 8 * 512], BF16)
        self.ab_sb = sb("ab_sb", [128, 4 * 512], BF16)
        self.ht_row = sb("ht_row", [1, 512], BF16)
        self.s2_sb = sb("s2_sb", [128, 1024])
        self.sqf = sb("sqf", [128, 512])
        self.becol = sb("becol", [128, 8])
        self.bshx = sb("bshx", [128, 8])
        self.be2row = sb("be2row", [1, 1024])
        self.w2time = sb("w2time", [1, 1024], BF16)
        self.colsc = sb("colsc", [128, 8])
        self.sc_t = sb("sc_t", [128, NE])
        self.eq_t = sb("eq_t", [128, NE])
        self.mk_t = sb("mk_t", [128, NE])
        self.mcol = sb("mcol", [128, 4])
        self.p_a = psa("p_a", [128, 512])
        self.p_b = psa("p_b", [128, 512])
        self.p_c = psa("p_c", [128, 512])
        self.p_d = psa("p_d", [128, 512])
        self.p_e = psa("p_e", [128, 512])
        self.p_f = psa("p_f", [128, 512])
        self.p_g = psa("p_g", [128, 512])
        self.p_tp = psa("p_tp", [128, 256], BF16)

        self.p0_consts()
        self.p1_ln()
        self.p2_weights()
        if self.with_gate:
            self.p3_gate()
        if self.with_chain:
            self.p4_chain()
        if self.with_post:
            self.p5_post()
        if self.with_ffn:
            self.p6_ffn()
        self.flush_dbg()
        self.g(lambda e: e.engine_nop(), deps=self.last_dma_out)
        trunc = getattr(self, "_trunc", None)
        if trunc is not None:
            keep = self.sch.ops[:trunc]
            self.sch.ops = keep
        self.sch.emit(nc)
        return nc

    def p0_consts(self):
        v, s, g, dma = self.v, self.s, self.g, self.dma
        d_id = dma("consts", self.ident_f[:, :], self.ident_h[:, :])
        o1 = g(lambda e: e.memset(self.ones_r[:, :], 1.0))
        o2 = g(lambda e: e.memset(self.ones_cf[:, :], 1.0))
        o3 = g(lambda e: e.memset(self.ones_cb[:, :], 1.0))
        o4 = g(lambda e: e.memset(self.idcell[:, :], 1.0))
        self.op_ones = [o1, o2, o3, o4]
        self.op_nw = dma("consts", self.wcol[:, :], self.nw_h[:, :])
        d_eta = dma("consts", self.etac[:, :], self.eta_h[:, :])
        a1 = s(lambda e: e.activation(out=self.etac[:, :], in_=self.etac[:, :],
                                      func=AF.Abs), deps=[d_eta])
        self.op_eta = v(lambda e: e.tensor_scalar_max(self.etac[:, :],
                                                      self.etac[:, :], ETA_MIN),
                        deps=[a1])
        self.op_identb = v(lambda e: e.tensor_copy(self.ident[:, :],
                                                   self.ident_f[:, :]),
                           deps=[d_id])

    def p1_ln(self):
        v, s, dma, dmat = self.v, self.s, self.dma, self.dmat
        T, NT = self.T, self.NT
        pu = {}
        self.xn_ts = []
        self.xr_ts = []
        for i in range(NT):
            ib = i % 2
            st_ = self.stt_[ib]
            dx = dma("gx", self.xt[:, :], self.x_hbm[i * 128:(i + 1) * 128, :],
                     deps=pu.get("xt", []))
            a_sq = s(lambda e, st_=st_: e.activation(
                out=self.xsq[:, :], in_=self.xt[:, :], func=AF.Square,
                accum_out=st_[:, 1:2]),
                deps=[dx] + pu.get("xsq", []) + pu.get(("st", ib), []), acc=True)
            r_sum = v(lambda e, st_=st_: e.reduce_sum(
                st_[:, 0:1], self.xt[:, :], axis=AX.X),
                deps=[dx] + pu.get(("st", ib), []))
            v(lambda e, st_=st_: e.tensor_scalar_mul(st_[:, 0:1], st_[:, 0:1], 1.0 / D))
            v(lambda e, st_=st_: e.tensor_scalar_mul(st_[:, 1:2], st_[:, 1:2], 1.0 / D),
              deps=[a_sq])
            v(lambda e, st_=st_: e.tensor_scalar(st_[:, 2:3], st_[:, 0:1],
                                                 st_[:, 0:1], None, ALU.mult))
            v(lambda e, st_=st_: e.tensor_sub(st_[:, 2:3], st_[:, 1:2], st_[:, 2:3]))
            v(lambda e, st_=st_: e.tensor_scalar_add(st_[:, 3:4], st_[:, 2:3], EPS_LN))
            rc = v(lambda e, st_=st_: e.reciprocal(st_[:, 4:5], st_[:, 3:4]))
            sq1 = s(lambda e, st_=st_: e.activation(out=st_[:, 5:6], in_=st_[:, 4:5],
                                                    func=AF.Sqrt), deps=[rc])
            nb = v(lambda e, st_=st_: e.tensor_scalar(
                st_[:, 6:7], st_[:, 0:1], st_[:, 5:6], -1.0, ALU.mult, ALU.mult),
                deps=[sq1])
            xn_op = s(lambda e, st_=st_: e.activation(
                out=self.xnt[:, :], in_=self.xt[:, :], func=AF.Identity,
                scale=st_[:, 5:6], bias=st_[:, 6:7]),
                deps=[nb] + pu.get("xnt", []))
            v(lambda e, st_=st_: e.tensor_tensor(
                st_[:, 7:8], st_[:, 2:3], st_[:, 4:5], ALU.mult))
            v(lambda e, st_=st_: e.tensor_scalar_add(st_[:, 7:8], st_[:, 7:8], EPS_RMS))
            rc2 = v(lambda e, st_=st_: e.reciprocal(st_[:, 7:8], st_[:, 7:8]))
            sq2 = s(lambda e, st_=st_: e.activation(out=st_[:, 7:8], in_=st_[:, 7:8],
                                                    func=AF.Sqrt), deps=[rc2])
            rs = v(lambda e, st_=st_: e.tensor_tensor(
                st_[:, 5:6], st_[:, 5:6], st_[:, 7:8], ALU.mult), deps=[sq2])
            nb2 = v(lambda e, st_=st_: e.tensor_tensor(
                st_[:, 6:7], st_[:, 6:7], st_[:, 7:8], ALU.mult))
            xr_op = s(lambda e, st_=st_: e.activation(
                out=self.xrt[:, :], in_=self.xt[:, :], func=AF.Identity,
                scale=st_[:, 5:6], bias=st_[:, 6:7]),
                deps=[nb2, rs] + pu.get("xrt", []))
            tn, sn = [], []
            for c in range(8):
                tp = dmat("tpn", self.stg[:, c * 128:(c + 1) * 128],
                          self.xnt[:, c * 128:(c + 1) * 128],
                          deps=[xn_op] + pu.get("stgn", []))
                tn.append(tp)
                sn.append(dma("tsn", self.xn_sc[c, :, i * 128:(i + 1) * 128],
                              self.stg[:, c * 128:(c + 1) * 128], deps=[tp]))
            tr, sr = [], []
            for c in range(8):
                tp = dmat("tpr", self.stg[:, c * 128:(c + 1) * 128],
                          self.xrt[:, c * 128:(c + 1) * 128],
                          deps=[xr_op, sn[c]])
                tr.append(tp)
                sr.append(dma("tsr", self.xnr_sc[c, :, i * 128:(i + 1) * 128],
                              self.stg[:, c * 128:(c + 1) * 128], deps=[tp]))
            self.xn_ts.append(sn)
            self.xr_ts.append(sr)
            pu["xt"] = [a_sq, r_sum, xn_op, xr_op]
            pu["xsq"] = [a_sq]
            pu[("st", ib)] = [xn_op, xr_op, nb2, rs]
            pu["xnt"] = tn
            pu["xrt"] = tr
            pu["stgn"] = sr

    def p2_weights(self):
        v, g, dma = self.v, self.g, self.dma
        wp = {}
        fold = [self.op_nw]

        def prep(dst, src_h, cols, foldw, extra_deps=()):
            lasts = []
            for c in range(8):
                dl = dma("wp", self.wtmp[:, 0:cols],
                         src_h[c * 128:(c + 1) * 128, :],
                         deps=wp.get("b", []) + list(extra_deps))
                if foldw:
                    cv = v(lambda e, c=c, cols=cols, dst=dst: e.tensor_scalar(
                        dst[:, c * cols:(c + 1) * cols], self.wtmp[:, 0:cols],
                        self.wcol[:, c:c + 1], None, ALU.mult), deps=[dl] + fold)
                else:
                    cv = v(lambda e, c=c, cols=cols, dst=dst: e.tensor_copy(
                        dst[:, c * cols:(c + 1) * cols], self.wtmp[:, 0:cols]),
                        deps=[dl])
                wp["b"] = [cv]
                lasts.append(cv)
            return lasts
        self.prep_w = prep
        self.wv_done = prep(self.wv_sb, self.wvT_h, D, True)
        self.wk_done = prep(self.wk_sb, self.wkT_h, F, True)
        self.wq_done = prep(self.wq_sb, self.wqT_h, F, True)
        self.gw_done = prep(self.gw_sb, self.gwT_h, NE, False)
        if not self.with_ffn:
            self.wffn_done = []
            return
        lastg = []
        for e_ix in range(NE):
            for src, ncols, nch, off in (
                    (self.we1T_h[e_ix], 512, 8, 0),
                    (self.we3T_h[e_ix], 512, 8, 4096),
                    (self.we2T_h[e_ix], 1023, 4, 8192)):
                for c in range(nch):
                    dl = dma("wp", self.wtmp[:, 0:ncols], src[c * 128:(c + 1) * 128, :],
                             deps=wp.get("b", []))
                    cast = g(lambda e, ncols=ncols: e.tensor_copy(
                        self.wstg[:, 0:ncols], self.wtmp[:, 0:ncols]),
                        deps=[dl] + lastg)
                    sd = dma("wfs", self.wffn_sc[e_ix, :, off + c * ncols:
                                                 off + (c + 1) * ncols],
                             self.wstg[:, 0:ncols], deps=[cast])
                    wp["b"] = [cast]
                    lastg = [sd]
        for src, ncols, nch, off in (
                (self.ws1T_h, 1024, 8, 0),
                (self.ws3T_h, 1024, 8, 8192),
                (self.ws2T_h, 1023, 8, 16384)):
            for c in range(nch):
                dl = dma("wp", self.wtmp[:, 0:ncols], src[c * 128:(c + 1) * 128, :],
                         deps=wp.get("b", []))
                cast = g(lambda e, ncols=ncols: e.tensor_copy(
                    self.wstg[:, 0:ncols], self.wtmp[:, 0:ncols]),
                    deps=[dl] + lastg)
                sd = dma("wfs", self.wsh_sc[:, off + c * ncols: off + (c + 1) * ncols],
                         self.wstg[:, 0:ncols], deps=[cast])
                wp["b"] = [cast]
                lastg = [sd]
        self.wffn_done = lastg

    def p3_gate(self):
        v, s, t, dma = self.v, self.s, self.t, self.dma
        NT = self.NT
        p = self.p_g
        prev = []
        mm_prev = []
        for i in range(NT):
            xnd = []
            for c in range(8):
                xnd.append(dma("gxn", self.stg[:, c * 128:(c + 1) * 128],
                               self.xn_sc[c, :, i * 128:(i + 1) * 128],
                               deps=[self.xn_ts[i][c]] + mm_prev))
            mms = []
            for c in range(8):
                mms.append(t(lambda e, c=c: e.matmul(
                    p[0:128, 0:NE], self.stg[:, c * 128:(c + 1) * 128],
                    self.gw_sb[:, c * NE:(c + 1) * NE],
                    start=(c == 0), stop=(c == 7)),
                    deps=[xnd[c]] + self.gw_done + prev))
            m1 = v(lambda e: e.reduce_max(self.mcol[:, 0:1], p[0:128, 0:NE],
                                          axis=AX.X), deps=[mms[-1]])
            nm = v(lambda e: e.tensor_scalar_mul(self.mcol[:, 0:1],
                                                 self.mcol[:, 0:1], -1.0))
            ex = s(lambda e: e.activation(out=self.sc_t[:, :], in_=p[0:128, 0:NE],
                                          func=AF.Exp, bias=self.mcol[:, 0:1]),
                   deps=[nm] + prev)
            sm = v(lambda e: e.reduce_sum(self.mcol[:, 1:2], self.sc_t[:, :],
                                          axis=AX.X), deps=[ex])
            v(lambda e: e.reciprocal(self.mcol[:, 1:2], self.mcol[:, 1:2]))
            sc = v(lambda e: e.tensor_scalar(self.sc_t[:, :], self.sc_t[:, :],
                                             self.mcol[:, 1:2], None, ALU.mult))
            m1b = v(lambda e: e.reduce_max(self.mcol[:, 2:3], self.sc_t[:, :],
                                           axis=AX.X))
            eq = v(lambda e: e.tensor_scalar(self.eq_t[:, :], self.sc_t[:, :],
                                             self.mcol[:, 2:3], None, ALU.is_equal))
            mk = v(lambda e: e.scalar_tensor_tensor(self.mk_t[:, :], self.eq_t[:, :],
                                                    -1e30, self.sc_t[:, :],
                                                    ALU.mult, ALU.add))
            m2 = v(lambda e: e.reduce_max(self.mcol[:, 3:4], self.mk_t[:, :],
                                          axis=AX.X))
            hit = v(lambda e: e.tensor_scalar(self.eq_t[:, :], self.sc_t[:, :],
                                              self.mcol[:, 3:4], None, ALU.is_ge))
            w = v(lambda e, i=i: e.tensor_tensor(self.w16[:, i * NE:(i + 1) * NE],
                                                 self.sc_t[:, :], self.eq_t[:, :],
                                                 ALU.mult))
            prev = [w, sc, eq, mk, hit, m1, sm, m1b, m2, ex]
            mm_prev = mms
        self.gate_done = prev
        self.dbg_add("w16", [128, self.NT * NE], lambda: self.w16[:, :], deps=prev)

    def p4_chain(self):
        v, s, t, g, dma = self.v, self.s, self.t, self.g, self.dma
        T, C, PB, NPB, CPP = self.T, self.C, self.PB, self.NPB, self.CPP
        st = self.st
        cell = lambda i: st[0:1, i:i + 1]
        init0 = [
            g(lambda e: e.memset(self.m_prev[:, :], 0.0)),
            g(lambda e: e.memset(self.m_bf[:, :], 0.0)),
            g(lambda e: e.memset(st[:, :], 0.0)),
            g(lambda e: e.memset(self.kapr[0:1, C:C + 1], 0.0)),
        ]
        init1 = [g(lambda e: e.memset(cell(3), 1.0), deps=init0)]
        self.chain_mbf_dep = init0 + init1
        prev_tail = init0 + init1 + getattr(self, "gate_done", [])
        vpb_done = {}
        for pb in range(NPB):
            tiles = range(pb * (PB // 128), (pb + 1) * (PB // 128))
            sc_deps = [self.xr_ts[i][c] for i in tiles for c in range(8)]
            xnr_d = []
            for c in range(8):
                xnr_d.append(dma("xpb", self.xnr_pb[:, c * PB:(c + 1) * PB],
                                 self.xnr_sc[c, :, pb * PB:(pb + 1) * PB],
                                 deps=sc_deps + vpb_done.get("xnr", [])))
            vops = []
            for ec in range(8):
                pban = self.p_a if ec % 2 == 0 else self.p_b
                mm = None
                for dc in range(8):
                    mm = t(lambda e, ec=ec, dc=dc, pban=pban: e.matmul(
                        pban[0:128, 0:PB],
                        self.wv_sb[:, dc * D + ec * 128: dc * D + (ec + 1) * 128],
                        self.xnr_pb[:, dc * PB:(dc + 1) * PB],
                        start=(dc == 0), stop=(dc == 7)),
                        deps=([xnr_d[dc]] + self.wv_done + vpb_done.get(ec % 2, [])
                              + prev_tail))
                ev = v(lambda e, ec=ec, pban=pban: e.tensor_copy(
                    self.v_pb[:, ec * PB:(ec + 1) * PB], pban[0:128, 0:PB]),
                    deps=[mm])
                vpb_done[ec % 2] = [ev]
                vops.append(ev)
            kmm = None
            for dc in range(8):
                kmm = t(lambda e, dc=dc: e.matmul(
                    self.p_c[0:F, 0:PB], self.wk_sb[:, dc * F:(dc + 1) * F],
                    self.xnr_pb[:, dc * PB:(dc + 1) * PB],
                    start=(dc == 0), stop=(dc == 7)),
                    deps=[xnr_d[dc]] + self.wk_done + vpb_done.get("k", [])
                    + prev_tail)
            mn = v(lambda e: e.tensor_single_scalar(self.pscr[:, 0:PB],
                                                    self.p_c[0:F, 0:PB], 0.0,
                                                    ALU.min),
                   deps=[kmm] + vpb_done.get("pscr", []))
            exk = s(lambda e: e.activation(out=self.pscr[:, 0:PB],
                                           in_=self.pscr[:, 0:PB], func=AF.Exp),
                    deps=[mn])
            kf = v(lambda e: e.scalar_tensor_tensor(self.k_pb[:, 0:PB],
                                                    self.p_c[0:F, 0:PB], 0.0,
                                                    self.pscr[:, 0:PB],
                                                    ALU.max, ALU.add), deps=[exk])
            kef = v(lambda e: e.tensor_scalar(self.ke_pb[:, 0:PB], self.k_pb[:, 0:PB],
                                              self.etac[:, 0:1], None, ALU.mult),
                    deps=[self.op_eta])
            kb1 = v(lambda e: e.tensor_copy(self.kb_pb[:, 0:PB], self.k_pb[:, 0:PB]))
            kb2 = v(lambda e: e.tensor_copy(self.keb_pb[:, 0:PB], self.ke_pb[:, 0:PB]))
            kops = [kf, kef, kb1, kb2]
            vpb_done["k"] = kops
            vpb_done["pscr"] = [kf]
            vpb_done["xnr"] = [kmm] + vops
            for cc in range(CPP):
                prev_tail = self.emit_chunk(cc, vops, kops, prev_tail)
        self.chain_done = prev_tail
        self.dbg_add("mfinal", [F, D], lambda: self.m_prev[:, :], deps=prev_tail)

    def emit_chunk(self, cc, vops, kops, prev_tail):
        stage = getattr(self, "_prep_stage", 99)
        if stage == 0:
            return prev_tail
        v, s, t, dma = self.v, self.s, self.t, self.dma
        C, PB = self.C, self.PB
        st = self.st
        cell = lambda i: st[0:1, i:i + 1]
        vsl = lambda ec: self.v_pb[:, ec * PB + cc * C: ec * PB + cc * C + C]
        kbsl = self.kb_pb[:, cc * C:cc * C + C]
        kebsl = self.keb_pb[:, cc * C:cc * C + C]
        kfsl = self.k_pb[:, cc * C:cc * C + C]
        kefsl = self.ke_pb[:, cc * C:cc * C + C]
        p_gr, p_A, p_B = self.p_e, self.p_c, self.p_d
        p_rw, p_r2 = self.p_g, self.p_f

        gv_mm = None
        for dc in range(8):
            gv_mm = t(lambda e, dc=dc: e.matmul(
                p_gr[0:C, 0:C], vsl(dc), vsl(dc), start=(dc == 0), stop=(dc == 7)),
                deps=[vops[dc]] + prev_tail)
        ht_mm = t(lambda e: e.matmul(p_gr[0:C, C:2 * C], kbsl, kebsl,
                                     start=True, stop=True), deps=kops + prev_tail)
        gk_mm = t(lambda e: e.matmul(p_gr[0:C, 3 * C:4 * C], kebsl, kebsl,
                                     start=True, stop=True), deps=kops)
        if stage == 1:
            gev = [
                v(lambda e: e.tensor_copy(self.evac64[:, 0:C], p_gr[0:C, C:2 * C]),
                  deps=[ht_mm, gv_mm, gk_mm] + prev_tail)]
            return prev_tail + gev
        a_mm, b_mm = [], []
        for dc in range(8):
            a_mm.append(t(lambda e, dc=dc: e.matmul(
                p_A[0:128, dc * C:(dc + 1) * C],
                self.m_bf[:, dc * 128:(dc + 1) * 128],
                kbsl, start=True, stop=True),
                deps=kops + self.chain_mbf_dep + prev_tail))
            b_mm.append(t(lambda e, dc=dc: e.matmul(
                p_B[0:128, dc * C:(dc + 1) * C],
                self.m_bf[:, dc * 128:(dc + 1) * 128],
                kebsl, start=True, stop=True),
                deps=kops + self.chain_mbf_dep + prev_tail))
        aev = []
        for dc in range(8):
            aev.append(v(lambda e, dc=dc: e.tensor_copy(
                self.a_bf[:, dc * C:(dc + 1) * C], p_A[0:128, dc * C:(dc + 1) * C]),
                deps=[a_mm[7]] + prev_tail))
        if stage == 21:
            return prev_tail + aev
        avt_mm = None
        for dc in range(8):
            avt_mm = t(lambda e, dc=dc: e.matmul(
                p_gr[0:C, 2 * C:3 * C], self.a_bf[:, dc * C:(dc + 1) * C], vsl(dc),
                start=(dc == 0), stop=(dc == 7)), deps=[aev[dc], vops[dc]])
        if stage == 22:
            fin = v(lambda e: e.tensor_copy(self.evac64[:, 0:C], p_gr[0:C, 2 * C:3 * C]),
                    deps=[avt_mm] + prev_tail)
            return prev_tail + aev + [fin]
        sqa = []
        for dc in range(8):
            sqa.append(s(lambda e, dc=dc: e.activation(
                out=self.sq_sb[:, dc * C:(dc + 1) * C],
                in_=p_A[0:128, dc * C:(dc + 1) * C], func=AF.Square),
                deps=[a_mm[7], aev[7]] + prev_tail))
        p2m = None
        for dc in range(8):
            p2m = t(lambda e, dc=dc: e.matmul(
                p_rw[0:1, 0:C], self.ones_cf[:, 0:1],
                self.sq_sb[:, dc * C:(dc + 1) * C],
                start=(dc == 0), stop=(dc == 7)),
                deps=[sqa[dc]] + self.op_ones + prev_tail)
        if stage == 23:
            fin = v(lambda e: e.tensor_copy(self.rowsc[0:1, 0:C], p_rw[0:1, 0:C]),
                    deps=[p2m] + prev_tail)
            return prev_tail + aev + [fin]
        p3v = []
        for dc in range(8):
            p3v.append(v(lambda e, dc=dc: e.tensor_tensor(
                self.sq2_sb[:, dc * C:(dc + 1) * C],
                p_A[0:128, dc * C:(dc + 1) * C], vsl(dc), ALU.mult),
                deps=[a_mm[7], vops[dc], sqa[7]] + prev_tail))
        p3m = None
        for dc in range(8):
            p3m = t(lambda e, dc=dc: e.matmul(
                p_rw[0:1, C:2 * C], self.ones_cf[:, 0:1],
                self.sq2_sb[:, dc * C:(dc + 1) * C],
                start=(dc == 0), stop=(dc == 7)), deps=[p3v[dc]])
        p4m = None
        for dc in range(8):
            p4m = t(lambda e, dc=dc: e.matmul(
                p_rw[0:1, 2 * C:3 * C], self.ones_cb[:, 0:1],
                self.a_bf[:, dc * C:(dc + 1) * C],
                start=(dc == 0), stop=(dc == 7)), deps=[aev[dc]] + self.op_ones)
        if stage == 24:
            fin = v(lambda e: e.tensor_copy(self.rowsc[0:1, 0:3 * C], p_rw[0:1, 0:3 * C]),
                    deps=[p2m, p3m, p4m] + prev_tail)
            return prev_tail + aev + [fin]
        sqv = []
        for dc in range(8):
            sqv.append(s(lambda e, dc=dc: e.activation(
                out=self.sq_sb[:, dc * C:(dc + 1) * C], in_=vsl(dc), func=AF.Square),
                deps=[vops[dc], p2m]))
        p1m = None
        for dc in range(8):
            p1m = t(lambda e, dc=dc: e.matmul(
                p_rw[0:1, 3 * C:4 * C], self.ones_cf[:, 0:1],
                self.sq_sb[:, dc * C:(dc + 1) * C],
                start=(dc == 0), stop=(dc == 7)), deps=[sqv[dc]])
        svm = None
        for dc in range(8):
            svm = t(lambda e, dc=dc: e.matmul(
                p_rw[0:1, 4 * C:5 * C], self.ones_cb[:, 0:1], vsl(dc),
                start=(dc == 0), stop=(dc == 7)),
                deps=[vops[dc]] + self.op_ones + prev_tail)
        bvv = []
        for dc in range(8):
            bvv.append(v(lambda e, dc=dc: e.tensor_tensor(
                self.sq2_sb[:, dc * C:(dc + 1) * C],
                p_B[0:128, dc * C:(dc + 1) * C], vsl(dc), ALU.mult),
                deps=[b_mm[7], vops[dc], p3m]))
        bvm = None
        for dc in range(8):
            bvm = t(lambda e, dc=dc: e.matmul(
                p_rw[0:1, 5 * C:6 * C], self.ones_cf[:, 0:1],
                self.sq2_sb[:, dc * C:(dc + 1) * C],
                start=(dc == 0), stop=(dc == 7)), deps=[bvv[dc]])
        kapm = t(lambda e: e.matmul(p_rw[0:1, 6 * C:7 * C], self.ones_cf[0:F, 0:1],
                                    kfsl, start=True, stop=True),
                 deps=kops + self.op_ones + prev_tail)
        kapem = t(lambda e: e.matmul(p_rw[0:1, 7 * C:8 * C], self.ones_cf[0:F, 0:1],
                                     kefsl, start=True, stop=True), deps=kops)
        ketam = t(lambda e: e.matmul(p_r2[0:1, 0:C], self.etac[:, 0:1], kfsl,
                                     start=True, stop=True),
                  deps=kops + [self.op_eta] + prev_tail)
        kesq = s(lambda e: e.activation(out=self.sq64[:, 0:C], in_=kefsl,
                                        func=AF.Square), deps=kops + prev_tail)
        kesqm = t(lambda e: e.matmul(p_r2[0:1, C:2 * C], self.ones_cf[0:F, 0:1],
                                     self.sq64[:, 0:C], start=True, stop=True),
                  deps=[kesq])
        if stage == 2:
            r_ev0 = v(lambda e: e.tensor_copy(self.rowsc[0:1, :], p_rw[0:1, 0:8 * C]),
                      deps=[p2m, p3m, p4m, p1m, svm, bvm, kapm, kapem, avt_mm,
                            gv_mm, ht_mm, gk_mm, ketam, kesqm])
            return prev_tail + [r_ev0]
        r_ev = v(lambda e: e.tensor_copy(self.rowsc[0:1, :], p_rw[0:1, 0:8 * C]),
                 deps=[p2m, p3m, p4m, p1m, svm, bvm, kapm, kapem])
        r_ev2 = v(lambda e: e.tensor_copy(self.rows2[0:1, 0:2 * C],
                                          p_r2[0:1, 0:2 * C]),
                  deps=[ketam, kesqm] + prev_tail)
        RW = self.rowsc
        P2s, P3s, P4s = RW[0:1, 0:C], RW[0:1, C:2 * C], RW[0:1, 2 * C:3 * C]
        P1s, SVs, BVs = (RW[0:1, 3 * C:4 * C], RW[0:1, 4 * C:5 * C],
                         RW[0:1, 5 * C:6 * C])
        KAPs, KAPEs = RW[0:1, 6 * C:7 * C], RW[0:1, 7 * C:8 * C]
        KETAs, KESQs = self.rows2[0:1, 0:C], self.rows2[0:1, C:2 * C]
        q6v = lambda j: self.q6r[0:1, :].rearrange("o (t k) -> o t k", k=6)[:, :, j]
        bkv = lambda j: self.bkr[0:1, :].rearrange("o (t k) -> o t k", k=2)[:, :, j]
        asm = [r_ev, r_ev2]
        A = asm.append
        A(v(lambda e: e.tensor_copy(q6v(0), P1s), deps=[r_ev]))
        A(v(lambda e: e.tensor_scalar_mul(q6v(1), P3s, -2.0), deps=[r_ev]))
        A(v(lambda e: e.tensor_tensor(q6v(2), KAPs, SVs, ALU.mult), deps=[r_ev]))
        A(v(lambda e: e.tensor_scalar_mul(q6v(2), q6v(2), -2.0)))
        A(v(lambda e: e.tensor_copy(q6v(3), P2s), deps=[r_ev]))
        A(v(lambda e: e.tensor_tensor(q6v(4), KAPs, P4s, ALU.mult), deps=[r_ev]))
        A(v(lambda e: e.tensor_scalar_mul(q6v(4), q6v(4), 2.0)))
        A(v(lambda e: e.tensor_tensor(q6v(5), KAPs, KAPs, ALU.mult), deps=[r_ev]))
        A(v(lambda e: e.tensor_scalar_mul(q6v(5), q6v(5), float(D))))
        A(v(lambda e: e.tensor_scalar_mul(bkv(0), BVs, 2.0 / FD), deps=[r_ev]))
        A(v(lambda e: e.tensor_tensor(bkv(1), KAPEs, SVs, ALU.mult), deps=[r_ev]))
        A(v(lambda e: e.tensor_scalar_mul(bkv(1), bkv(1), 2.0 / FD)))
        A(v(lambda e: e.tensor_tensor(self.mhatr[0:1, :], KETAs, SVs, ALU.mult),
            deps=[r_ev, r_ev2]))
        A(v(lambda e: e.tensor_scalar_mul(self.mhatr[0:1, :], self.mhatr[0:1, :],
                                          1.0 / FD)))
        A(v(lambda e: e.tensor_tensor(self.aamr[0:1, :], KESQs, P1s, ALU.mult),
            deps=[r_ev, r_ev2]))
        A(v(lambda e: e.tensor_scalar_mul(self.aamr[0:1, :], self.aamr[0:1, :],
                                          1.0 / FD)))
        A(v(lambda e: e.tensor_tensor(self.scr[0:1, :], self.mhatr[0:1, :],
                                      self.mhatr[0:1, :], ALU.mult)))
        A(v(lambda e: e.tensor_sub(self.aamr[0:1, :], self.aamr[0:1, :],
                                   self.scr[0:1, :])))
        A(v(lambda e: e.tensor_copy(self.kapr[0:1, 0:C], KAPs), deps=[r_ev]))
        grw = [ht_mm, gv_mm, avt_mm, gk_mm]
        gev = [
            v(lambda e: e.tensor_copy(self.evac64[:, 0:C], p_gr[0:C, C:2 * C]),
              deps=grw + prev_tail),
            v(lambda e: e.tensor_copy(self.evac64[:, C:2 * C], p_gr[0:C, 0:C]),
              deps=grw + prev_tail),
            v(lambda e: e.tensor_copy(self.evac64[:, 2 * C:3 * C],
                                      p_gr[0:C, 2 * C:3 * C]),
              deps=grw + prev_tail),
            v(lambda e: e.tensor_scalar_mul(self.evac64[:, 3 * C:4 * C],
                                            p_gr[0:C, 3 * C:4 * C], 2.0 / FD),
              deps=grw + prev_tail),
        ]
        flv = lambda dst: dst[0:1, :].rearrange("o (p f) -> o p f", p=C)
        fl = [
            dma("fl", flv(self.htf), self.evac64[:, 0:C], deps=[gev[0]] + prev_tail),
            dma("fl", flv(self.gvtf), self.evac64[:, C:2 * C],
                deps=[gev[1]] + prev_tail),
            dma("fl", flv(self.avttf), self.evac64[:, 2 * C:3 * C],
                deps=[gev[2]] + prev_tail),
            dma("fl", flv(self.gktf), self.evac64[:, 3 * C:4 * C],
                deps=[gev[3]] + prev_tail),
        ]
        sinit = [
            v(lambda e: e.memset(self.phr[0:1, :], 0.0), deps=prev_tail),
            v(lambda e: e.memset(st[0:1, 4:9], 0.0), deps=prev_tail),
            v(lambda e: e.memset(st[0:1, 25:26], 0.0), deps=prev_tail),
        ]
        sinit.append(v(lambda e: e.memset(cell(4), 1.0)))
        sinit.append(v(lambda e: e.memset(cell(6), 1.0)))
        prep_done = asm + gev + fl + sinit + [kesqm]
        if stage == 3:
            return prep_done
        nsteps = getattr(self, "_nsteps", None)
        last = self.emit_steps(cc, prep_done, nsteps)
        if stage == 4:
            return last
        return self.emit_mupdate(cc, last, vops)

    def emit_steps(self, cc, prep_done, nsteps=None):
        v, s, t = self.v, self.s, self.t
        C = self.C
        st = self.st
        I_ONE, I_U, I_E = 3, 4, 5
        I_U2, I_E2 = 6, 8
        I_AL, I_LIN, I_WSQ = 10, 11, 12
        I_C2, I_CC = 13, 14
        I_YD, I_G1B, I_G12, I_T1, I_VAR, I_SIG2 = 15, 16, 17, 18, 19, 20
        I_IS2, I_ISG, I_NU, I_MU, I_EK = 21, 22, 23, 24, 25
        cell = lambda i: st[0:1, i:i + 1]
        htf, gvtf, avttf, gktf = self.htf, self.gvtf, self.avttf, self.gktf
        phr, ghr, gher = self.phr, self.ghr, self.gher
        q1r, q2r, scr, scr6 = self.q1r, self.q2r, self.scr, self.scr6
        p_ch = self.p_f
        row = lambda tf, tt: tf[0:1, tt * C:(tt + 1) * C]
        last_ops = prep_done
        for tt in range(C if nsteps is None else min(C, nsteps)):
            dep0 = prep_done if tt == 0 else []
            a1 = v(lambda e, tt=tt: e.tensor_tensor(ghr[0:1, :], phr[0:1, :],
                                                    row(htf, tt), ALU.mult),
                   deps=dep0)
            a2 = v(lambda e, tt=tt: e.tensor_tensor(gher[0:1, :], phr[0:1, :],
                                                    row(gktf, tt), ALU.mult))
            t1 = t(lambda e: e.matmul(p_ch[0:C, 200:201], ghr[0:1, 0:C],
                                      self.idcell[0:1, 0:1], start=True, stop=True),
                   deps=[a1] + self.op_ones + dep0)
            v(lambda e, tt=tt: e.scalar_tensor_tensor(
                q1r[0:1, :], row(avttf, tt), cell(I_U), row(gvtf, tt),
                ALU.mult, ALU.subtract))
            v(lambda e: e.scalar_tensor_tensor(
                q2r[0:1, :], self.rowsc[0:1, 4 * C:5 * C], cell(I_EK), q1r[0:1, :],
                ALU.mult, ALU.add))
            a5 = v(lambda e: e.scalar_tensor_tensor(
                scr[0:1, :], q2r[0:1, :], 2.0, ghr[0:1, :], ALU.mult, ALU.mult,
                accum_out=cell(I_LIN)))
            gce = v(lambda e: e.tensor_copy(self.gcol[:, 0:1], p_ch[0:C, 200:201]),
                    deps=[t1])
            t2 = t(lambda e: e.matmul(p_ch[0:1, 256:256 + C], self.gcol[:, 0:1],
                                      self.evac64[:, C:2 * C], start=True,
                                      stop=True),
                   deps=[gce])
            a6 = v(lambda e, tt=tt: e.scalar_tensor_tensor(
                scr6[0:1, 0:6], self.q6r[0:1, 6 * tt:6 * tt + 6], 1.0,
                st[0:1, I_ONE:I_ONE + 6], ALU.mult, ALU.mult, accum_out=cell(I_AL)))
            a7 = v(lambda e, tt=tt: e.scalar_tensor_tensor(
                scr[0:1, :], row(gvtf, tt), 1.0, gher[0:1, :], ALU.mult, ALU.mult,
                accum_out=cell(I_YD)), deps=[a2])
            a8 = v(lambda e, tt=tt: e.scalar_tensor_tensor(
                scr6[0:1, 6:8], self.bkr[0:1, 2 * tt:2 * tt + 2], 1.0,
                st[0:1, I_U:I_U + 2], ALU.mult, ALU.mult, accum_out=cell(I_G1B)))
            a9 = v(lambda e: e.reduce_sum(cell(I_G12), st[0:1, I_YD:I_YD + 2],
                                          axis=AX.X), deps=[a7, a8])
            a10 = v(lambda e: e.scalar_tensor_tensor(
                scr[0:1, :], p_ch[0:1, 256:256 + C], 1.0, ghr[0:1, :],
                ALU.mult, ALU.mult, accum_out=cell(I_WSQ)), deps=[t2])
            a11 = v(lambda e: e.reduce_sum(cell(I_C2), st[0:1, I_AL:I_AL + 3],
                                           axis=AX.X), deps=[a5, a6, a10])
            a12 = v(lambda e: e.tensor_single_scalar(cell(I_C2), cell(I_C2), 0.0,
                                                     ALU.max))
            s1 = s(lambda e: e.activation(out=cell(I_CC), in_=cell(I_C2),
                                          func=AF.Sqrt), deps=[a12])
            v(lambda e, tt=tt: e.tensor_scalar(
                cell(I_T1), cell(I_CC), self.aamr[0:1, tt:tt + 1], cell(I_G12),
                ALU.mult, ALU.add), deps=[s1, a9])
            v(lambda e: e.tensor_scalar(cell(I_VAR), cell(I_T1), cell(I_CC),
                                        cell(I_NU), ALU.mult, ALU.add))
            v(lambda e: e.tensor_single_scalar(cell(I_SIG2), cell(I_VAR),
                                               EPS_LN, ALU.add))
            b4 = v(lambda e: e.reciprocal(cell(I_IS2), cell(I_SIG2)))
            s2 = s(lambda e: e.activation(out=cell(I_ISG), in_=cell(I_IS2),
                                          func=AF.Sqrt), deps=[b4])
            b5 = v(lambda e, tt=tt: e.tensor_scalar(
                cell(I_MU), cell(I_CC), self.mhatr[0:1, tt:tt + 1], None, ALU.mult),
                deps=[s1])
            c1 = v(lambda e: e.tensor_scalar(phr[0:1, :], phr[0:1, :], cell(I_ISG),
                                             None, ALU.mult), deps=[s2])
            c2 = v(lambda e, tt=tt: e.tensor_scalar(
                phr[0:1, tt:tt + 1], cell(I_CC), cell(I_ISG), None, ALU.mult))
            c3 = v(lambda e: e.tensor_scalar(cell(I_U), cell(I_U), cell(I_ISG),
                                             None, ALU.mult))
            v(lambda e: e.tensor_scalar(cell(I_E), cell(I_MU), -1.0, cell(I_E),
                                        ALU.mult, ALU.add), deps=[b5])
            c5 = v(lambda e: e.tensor_scalar(cell(I_E), cell(I_E), cell(I_ISG),
                                             None, ALU.mult))
            c6 = v(lambda e: e.tensor_scalar(cell(I_NU), cell(I_VAR), cell(I_IS2),
                                             None, ALU.mult))
            c7 = v(lambda e: e.tensor_scalar(st[0:1, I_U2:I_U2 + 2],
                                             st[0:1, I_U:I_U + 2], cell(I_U),
                                             None, ALU.mult))
            c8 = v(lambda e: e.tensor_scalar(cell(I_E2), cell(I_E), cell(I_E),
                                             None, ALU.mult))
            c9 = v(lambda e, tt=tt: e.tensor_scalar(
                cell(I_EK), cell(I_E), self.kapr[0:1, tt + 1:tt + 2], None,
                ALU.mult))
            last_ops = [c1, c2, c3, c5, c6, c7, c8, c9, s1, s2, a10, gce, t2]
        return last_ops

    def emit_mupdate(self, cc, last, vops):
        v, t = self.v, self.t
        C, PB = self.C, self.PB
        st = self.st
        I_U = 4
        vsl = lambda ec: self.v_pb[:, ec * PB + cc * C: ec * PB + cc * C + C]
        phb = t(lambda e: e.matmul(self.p_f[0:C, 128:128 + C], self.ones_r[0:1, 0:C],
                                   self.phr[0:1, 0:C], start=True, stop=True),
                deps=last + self.op_ones)
        kphv = v(lambda e, cc=cc: e.tensor_tensor(
            self.kph[:, 0:C], self.ke_pb[:, cc * C:cc * C + C],
            self.p_f[0:F, 128:128 + C], ALU.mult), deps=[phb])
        kpht = t(lambda e: e.transpose(self.p_tp[0:C, 0:F], self.kph[:, 0:C],
                                       self.ident[0:F, 0:F]),
                 deps=[kphv, self.op_identb])
        kphe = v(lambda e: e.tensor_copy(self.kphT[:, 0:F], self.p_tp[0:C, 0:F]),
                 deps=[kpht])
        vt_ops = []
        for dc in range(8):
            sl = (dc % 2) * 128
            war = vt_ops[-1:] if vt_ops else [kphe]
            tp = t(lambda e, dc=dc, sl=sl: e.transpose(
                self.p_tp[0:C, sl:sl + 128], vsl(dc)[:, :], self.ident[:, :]),
                deps=[vops[dc], self.op_identb] + war + last)
            ev = v(lambda e, dc=dc, sl=sl: e.tensor_copy(
                self.vtok[:, dc * 128:(dc + 1) * 128], self.p_tp[0:C, sl:sl + 128]),
                deps=[tp])
            vt_ops.append(ev)
        mads = []
        for half in range(2):
            mads.append(t(lambda e, half=half: e.matmul(
                (self.p_a if half == 0 else self.p_b)[0:F, 0:512],
                self.kphT[:, 0:F], self.vtok[:, half * 512:(half + 1) * 512],
                start=True, stop=True), deps=[kphe] + vt_ops + last))
        ueb = t(lambda e: e.matmul(self.p_c[0:F, 300:302], self.ones_r[0:1, 0:F],
                                   st[0:1, I_U:I_U + 2], start=True, stop=True),
                deps=last + self.op_ones)
        uee = v(lambda e: e.tensor_copy(self.uebc[:, 0:2], self.p_c[0:F, 300:302]),
                deps=[ueb])
        mnew = []
        for half in range(2):
            v(lambda e, half=half: e.scalar_tensor_tensor(
                self.m_prev[:, half * 512:(half + 1) * 512],
                self.m_prev[:, half * 512:(half + 1) * 512], self.uebc[:, 0:1],
                (self.p_a if half == 0 else self.p_b)[0:F, 0:512],
                ALU.mult, ALU.add), deps=[uee, mads[half]] + self.chain_mbf_dep)
            v(lambda e, half=half: e.tensor_scalar(
                self.m_prev[:, half * 512:(half + 1) * 512],
                self.m_prev[:, half * 512:(half + 1) * 512], self.uebc[:, 1:2],
                None, ALU.add))
            m3 = v(lambda e, half=half: e.tensor_copy(
                self.m_bf[:, half * 512:(half + 1) * 512],
                self.m_prev[:, half * 512:(half + 1) * 512]))
            mnew.append(m3)
        self.chain_mbf_dep = mnew
        return mnew + [kphe, uee] + last

    def p5_post(self):
        v, s, t, dma, dmat = self.v, self.s, self.t, self.dma, self.dmat
        T, PB, NPB = self.T, self.PB, self.NPB
        chain_tail = list(getattr(self, "chain_done", []))
        wo_done = self.prep_w(self.wv_sb, self.woT_h, D, False,
                              extra_deps=chain_tail)
        prev = {}
        self.s_sc_ops = []
        mo = self.v_pb
        sfm = self.xnr_pb
        q_pb, qb_pb = self.k_pb, self.kb_pb
        for pb in range(NPB):
            tiles = range(pb * (PB // 128), (pb + 1) * (PB // 128))
            sc_deps = [self.xr_ts[i][c] for i in tiles for c in range(8)]
            xnr_d = []
            for c in range(8):
                xnr_d.append(dma("xpq", self.xnr_pb[:, c * PB:(c + 1) * PB],
                                 self.xnr_sc[c, :, pb * PB:(pb + 1) * PB],
                                 deps=sc_deps + chain_tail + prev.get("xnr", [])))
            qmm = None
            for dc in range(8):
                qmm = t(lambda e, dc=dc: e.matmul(
                    self.p_c[0:F, 0:PB], self.wq_sb[:, dc * F:(dc + 1) * F],
                    self.xnr_pb[:, dc * PB:(dc + 1) * PB],
                    start=(dc == 0), stop=(dc == 7)),
                    deps=[xnr_d[dc]] + self.wq_done + chain_tail + prev.get("q", [])
                    )
            mn = v(lambda e: e.tensor_single_scalar(self.pscr[:, 0:PB],
                                                    self.p_c[0:F, 0:PB], 0.0,
                                                    ALU.min),
                   deps=[qmm] + chain_tail + prev.get("pscr", []))
            exq = s(lambda e: e.activation(out=self.pscr[:, 0:PB],
                                           in_=self.pscr[:, 0:PB], func=AF.Exp),
                    deps=[mn] + chain_tail)
            qf = v(lambda e: e.scalar_tensor_tensor(q_pb[:, 0:PB],
                                                    self.p_c[0:F, 0:PB], 0.0,
                                                    self.pscr[:, 0:PB],
                                                    ALU.max, ALU.add),
                   deps=[exq] + chain_tail)
            qb = v(lambda e: e.tensor_copy(qb_pb[:, 0:PB], q_pb[:, 0:PB]),
                   deps=chain_tail)
            prev["q"] = [mn, qf, qb]
            prev["pscr"] = [qf]
            moe = []
            for dc in range(8):
                pban = self.p_a if dc % 2 == 0 else self.p_b
                mm = t(lambda e, dc=dc, pban=pban: e.matmul(
                    pban[0:128, 0:PB], self.m_bf[:, dc * 128:(dc + 1) * 128],
                    qb_pb[:, 0:PB], start=True, stop=True),
                    deps=[qb] + chain_tail + self.chain_mbf_dep +
                    prev.get(("mo", dc % 2), []))
                ev = v(lambda e, dc=dc, pban=pban: e.tensor_copy(
                    mo[:, dc * PB:(dc + 1) * PB], pban[0:128, 0:PB]),
                    deps=[mm] + ([qmm] if dc < 2 else []))
                prev[("mo", dc % 2)] = [ev]
                moe.append(ev)
            sfm_ops = []
            for ec in range(8):
                pban = self.p_d if ec % 2 == 0 else self.p_e
                mm = None
                for dc in range(8):
                    mm = t(lambda e, ec=ec, dc=dc, pban=pban: e.matmul(
                        pban[0:128, 0:PB],
                        self.wv_sb[:, dc * D + ec * 128: dc * D + (ec + 1) * 128],
                        mo[:, dc * PB:(dc + 1) * PB],
                        start=(dc == 0), stop=(dc == 7)),
                        deps=[moe[dc]] + wo_done + chain_tail +
                        prev.get(("s", ec % 2), []))
                xnd = dma("sxn", self.xnsl[:, 0:PB],
                          self.xn_sc[ec, :, pb * PB:(pb + 1) * PB],
                          deps=[self.xn_ts[i][ec] for i in tiles] +
                          prev.get("sxn", []))
                ev = v(lambda e, ec=ec, pban=pban: e.tensor_tensor(
                    sfm[:, ec * PB:(ec + 1) * PB], self.xnsl[:, 0:PB],
                    pban[0:128, 0:PB], ALU.add),
                    deps=[mm, xnd] + ([qmm] if ec < 2 else []))
                prev[("s", ec % 2)] = [ev]
                prev["sxn"] = [ev]
                sfm_ops.append(ev)
            strp_prev = prev.get("strp", [])
            str_ops = []
            for j in range(PB // 128):
                trs = []
                for ec in range(8):
                    trs.append(dmat("str", self.stg[:, ec * 128:(ec + 1) * 128],
                                    sfm[:, ec * PB + j * 128:
                                        ec * PB + (j + 1) * 128],
                                    deps=[sfm_ops[ec]] + strp_prev))
                sto = dma("sts",
                          self.s_sc[pb * PB + j * 128: pb * PB + (j + 1) * 128, :],
                          self.stg[:, :], deps=trs)
                strp_prev = [sto]
                str_ops.append(sto)
                self.s_sc_ops.append(sto)
            prev["strp"] = strp_prev
            prev["xnr"] = [qmm] + moe + sfm_ops
        self.dbg_add("sfm", [128, 8 * PB], lambda: sfm[:, :],
                     deps=sfm_ops + str_ops, d=BF16)

    def p6_ffn(self):
        v, s, t, g, dma = self.v, self.s, self.t, self.g, self.dma
        T, QT = self.T, self.QT
        PIECE = self.PIECE
        prev = {}
        bufdeps = {0: [], 1: []}
        post_tail = (list(getattr(self, "s_sc_ops", []))
                     + getattr(self, "gate_done", []))
        out_ops = []
        yv = self.y_q[0:128, :].rearrange("p (j d) -> p j d", d=D)
        nxtbuf = 0
        for q in range(self.NQ):
            q0 = q * QT
            xnq = []
            tiles = range(q0 // 128, (q0 + QT) // 128)
            for c in range(8):
                xnq.append(dma("fxn", self.xnsl[:, c * QT:(c + 1) * QT],
                               self.xn_sc[c, :, q0:q0 + QT],
                               deps=[self.xn_ts[i][c] for i in tiles] +
                               prev.get("xnq", []) + post_tail))
            yi1 = g(lambda e: e.memset(self.y_q[:, :], 0.0),
                    deps=prev.get("y", []) + post_tail)
            yi2 = g(lambda e: e.memset(yv[:, :, 0:1], 1.0), deps=[yi1])
            ydep = [yi2]
            gsl = None
            last_xn_mm = []
            for e_ix in range(NE + 1):
                shared = (e_ix == NE)
                ncol1 = 8192 if shared else 4096
                ncol2 = 8 * 1023 if shared else 4 * 1023
                off3 = ncol1
                off2 = 2 * ncol1
                srcw = self.wsh_sc if shared else self.wffn_sc[e_ix]
                w1b = nxtbuf
                pc1 = dma("wl%d" % w1b, self.wff[:, w1b * PIECE: w1b * PIECE + ncol1],
                          srcw[:, 0:ncol1],
                          deps=(self.wffn_done if q == 0 else []) + bufdeps[w1b])
                w3b = 1 - w1b
                pc3 = dma("wl%d" % w3b, self.wff[:, w3b * PIECE: w3b * PIECE + ncol1],
                          srcw[:, off3:off3 + ncol1],
                          deps=(self.wffn_done if q == 0 else []) + bufdeps[w3b])
                if shared:
                    sksp = 1024
                else:
                    sksp = 512
                wsl1 = lambda c, m, w1b=w1b, sksp=sksp: self.wff[
                    :, w1b * PIECE + c * sksp + m * 128:
                    w1b * PIECE + c * sksp + (m + 1) * 128]
                wsl3 = lambda c, m, w3b=w3b, sksp=sksp: self.wff[
                    :, w3b * PIECE + c * sksp + m * 128:
                    w3b * PIECE + c * sksp + (m + 1) * 128]
                b1d = dma("bl", self.becol[:, 0:4],
                          self.bs1_h[:, 0:4] if shared else self.be1_h[e_ix, :, :],
                          deps=prev.get("bc", []))
                b3d = dma("bl", self.becol[:, 4:8],
                          self.bs3_h[:, 0:4] if shared else self.be3_h[e_ix, :, :],
                          deps=prev.get("bc", []))
                bx = []
                if shared:
                    bx = [dma("bl", self.bshx[:, 0:4], self.bs1_h[:, 4:8],
                              deps=prev.get("bc2", [])),
                          dma("bl", self.bshx[:, 4:8], self.bs3_h[:, 4:8],
                              deps=prev.get("bc2", []))]
                b2d = dma("bl", self.be2row[0:1, 0:1023],
                          self.bs2_h[0:1, :] if shared else self.be2_h[e_ix, 0:1, :],
                          deps=prev.get("bc", []))
                wtd = dma("bl", self.wtmp[0:1, 0:1023],
                          self.ws2ti_h[0:1, :] if shared else self.w2ti_h[e_ix, 0:1, :],
                          deps=prev.get("w2t", []))
                wtc = v(lambda e: e.tensor_copy(self.w2time[0:1, 0:1023],
                                                self.wtmp[0:1, 0:1023]),
                        deps=[wtd] + prev.get("w2tc", []))
                bdeps = [b1d, b3d, b2d, wtc] + bx
                NM = 8 if shared else 4
                NKH = 8 if shared else 4
                NFE = 1023
                l1ev, l3ev = {}, {}
                pb_l = [self.p_a, self.p_b, self.p_c, self.p_d]
                l1mm_last = {}
                for mpass in range(NM // 4):
                    for mi in range(4):
                        m = mpass * 4 + mi
                        mm = None
                        for c in range(8):
                            mm = t(lambda e, c=c, m=m, mi=mi, wsl1=wsl1: e.matmul(
                                pb_l[mi][0:128, 0:QT], wsl1(c, m),
                                self.xnsl[:, c * QT:(c + 1) * QT],
                                start=(c == 0), stop=(c == 7)),
                                deps=[pc1, xnq[c]] + prev.get(("l1", mi), [])
                                + post_tail)
                        l1mm_last[mi] = mm
                        sev = s(lambda e, m=m, mi=mi: e.activation(
                            out=self.ab_sb[:, mi * 512:mi * 512 + QT],
                            in_=pb_l[mi][0:128, 0:QT], func=AF.Silu,
                            bias=(self.becol[:, m:m + 1] if m < 4
                                  else self.bshx[:, m - 4:m - 3])),
                            deps=[mm] + bdeps)
                        l1ev[m] = sev
                    for mi in range(4):
                        m = mpass * 4 + mi
                        mm = None
                        for c in range(8):
                            mm = t(lambda e, c=c, m=m, mi=mi, wsl3=wsl3: e.matmul(
                                pb_l[mi][0:128, 0:QT], wsl3(c, m),
                                self.xnsl[:, c * QT:(c + 1) * QT],
                                start=(c == 0), stop=(c == 7)),
                                deps=[pc3, l1ev[m]])
                        last_xn_mm = [mm]
                        hev = v(lambda e, m=m, mi=mi: e.scalar_tensor_tensor(
                            self.h_sb[:, m * 512:m * 512 + QT],
                            pb_l[mi][0:128, 0:QT],
                            (self.becol[:, 4 + m:5 + m] if m < 4
                             else self.bshx[:, m:m + 1]),
                            self.ab_sb[:, mi * 512:mi * 512 + QT],
                            ALU.add, ALU.mult),
                            deps=[mm, l1ev[m]] + bdeps + prev.get("huse", []))
                        l3ev[m] = hev
                        prev[("l1", mi)] = [hev]
                pc2 = dma("wl%d" % w1b, self.wff[:, w1b * PIECE: w1b * PIECE + ncol2],
                          srcw[:, off2:off2 + ncol2],
                          deps=(self.wffn_done if q == 0 else []) +
                          list(l1mm_last.values()))
                w2b = w1b
                wsl2 = lambda c, n0, n1, w2b=w2b: self.wff[
                    :, w2b * PIECE + c * 1023 + n0: w2b * PIECE + c * 1023 + n1]
                nxtbuf = w3b
                bufdeps[w3b] = []
                sqp = []
                for m in range(NM):
                    sq = s(lambda e, m=m: e.activation(
                        out=self.sqf[:, 0:QT],
                        in_=self.h_sb[:, m * 512:m * 512 + QT],
                        func=AF.Square), deps=[l3ev[m]] + sqp[-1:])
                    sqm = t(lambda e, m=m, NM=NM: e.matmul(
                        self.p_e[0:1, 0:QT], self.ones_cf[:, 0:1],
                        self.sqf[:, 0:QT],
                        start=(m == 0), stop=(m == NM - 1)),
                        deps=[sq] + self.op_ones + prev.get("tsq", []) +
                        prev.get("l2e", []))
                    sqp.append(sqm)
                htw = s(lambda e: e.activation(out=self.ht_row[0:1, 0:QT],
                                               in_=self.p_e[0:1, 0:QT],
                                               func=AF.Sqrt, bias=1.0),
                        deps=[sqp[-1]])
                prev["tsq"] = [htw]
                for m in range(QT // 128):
                    jj = m
                    tile_ix = q0 // 128 + m
                    mmlo = mmhi = None
                    for c in range(NKH):
                        mmlo = t(lambda e, c=c, m=m, wsl2=wsl2: e.matmul(
                            self.p_e[0:128, 0:512],
                            self.h_sb[:, c * 512 + m * 128: c * 512 + (m + 1) * 128],
                            wsl2(c, 0, 512), start=(c == 0), stop=False),
                            deps=[l3ev[c], pc2] + prev.get("l2e", []) +
                            ([htw] if c == 0 else []))
                        mmhi = t(lambda e, c=c, m=m, wsl2=wsl2: e.matmul(
                            self.p_f[0:128, 0:NFE - 512],
                            self.h_sb[:, c * 512 + m * 128: c * 512 + (m + 1) * 128],
                            wsl2(c, 512, NFE), start=(c == 0), stop=False),
                            deps=[l3ev[c], pc2] + prev.get("l2e", []))
                    tlo = t(lambda e, m=m: e.matmul(
                        self.p_e[0:128, 0:512],
                        self.ht_row[0:1, m * 128:(m + 1) * 128],
                        self.w2time[0:1, 0:512], start=False, stop=False),
                        deps=[htw, wtc])
                    thi = t(lambda e, m=m: e.matmul(
                        self.p_f[0:128, 0:NFE - 512],
                        self.ht_row[0:1, m * 128:(m + 1) * 128],
                        self.w2time[0:1, 512:NFE], start=False, stop=False),
                        deps=[htw, wtc])
                    blo = t(lambda e, m=m: e.matmul(
                        self.p_e[0:128, 0:512],
                        self.ones_r[0:1, 0:128],
                        self.be2row[0:1, 0:512], start=False, stop=True),
                        deps=bdeps + self.op_ones)
                    bhi = t(lambda e, m=m: e.matmul(
                        self.p_f[0:128, 0:NFE - 512],
                        self.ones_r[0:1, 0:128],
                        self.be2row[0:1, 512:NFE], start=False, stop=True),
                        deps=bdeps + self.op_ones)
                    ysl = yv[:, jj, :]
                    yspc = self.y_q[:, jj * D + 1: jj * D + D]
                    ycol = self.y_q[:, jj * D: jj * D + 1]
                    wc = (1.0 if shared else
                          self.w16[:, tile_ix * NE + e_ix: tile_ix * NE + e_ix + 1])
                    s2e1 = v(lambda e: e.tensor_copy(self.s2_sb[:, 0:512],
                                                     self.p_e[0:128, 0:512]),
                             deps=[blo, tlo, mmlo])
                    s2e2 = v(lambda e: e.tensor_copy(self.s2_sb[:, 512:NFE],
                                                     self.p_f[0:128, 0:NFE - 512]),
                             deps=[bhi, thi, mmhi])
                    zt1 = s(lambda e: e.activation(
                        out=self.sqf[:, 0:512], in_=self.s2_sb[:, 0:512],
                        func=AF.Square, accum_out=self.colsc[:, 0:1]),
                        deps=[s2e1, htw], acc=True)
                    zt2 = s(lambda e: e.activation(
                        out=self.sqf[:, 0:NFE - 512], in_=self.s2_sb[:, 512:NFE],
                        func=AF.Square, accum_out=self.colsc[:, 1:2]),
                        deps=[s2e2, zt1], acc=True)
                    prev["l2e"] = [s2e1, s2e2, zt1, zt2]
                    t2s = v(lambda e: e.tensor_add(self.colsc[:, 2:3],
                                                   self.colsc[:, 0:1],
                                                   self.colsc[:, 1:2]),
                            deps=[zt1, zt2])
                    t2c = s(lambda e: e.activation(out=self.colsc[:, 3:4],
                                                   in_=self.colsc[:, 2:3],
                                                   func=AF.Sqrt, bias=1.0),
                            deps=[t2s])
                    z1 = v(lambda e, wc=wc, yspc=yspc: e.scalar_tensor_tensor(
                        yspc, self.s2_sb[:, 0:NFE], wc, yspc, ALU.mult, ALU.add),
                        deps=[s2e1, s2e2] + ydep +
                        (getattr(self, "gate_done", [])[:1] if not shared else []))
                    z2 = v(lambda e, wc=wc, ycol=ycol: e.scalar_tensor_tensor(
                        ycol, self.colsc[:, 3:4], wc, ycol, ALU.mult, ALU.add),
                        deps=[t2c] + ydep)
                    zq = s(lambda e, ysl=ysl: e.activation(
                        out=self.s2_sb[:, 0:1024], in_=ysl, func=AF.Square,
                        accum_out=self.colsc[:, 4:5]), deps=[z1, z2], acc=True)
                    v(lambda e, ycol=ycol: e.tensor_scalar(
                        self.colsc[:, 5:6], ycol, ycol, 2.0, ALU.mult, ALU.mult),
                        deps=[z1, z2])
                    v(lambda e: e.tensor_sub(self.colsc[:, 5:6], self.colsc[:, 5:6],
                                             self.colsc[:, 4:5]), deps=[zq])
                    v(lambda e: e.tensor_scalar_max(self.colsc[:, 5:6],
                                                    self.colsc[:, 5:6], 1e-6))
                    i2d = v(lambda e: e.reciprocal(self.colsc[:, 6:7],
                                                   self.colsc[:, 5:6]))
                    iscl = s(lambda e: e.activation(out=self.colsc[:, 7:8],
                                                    in_=self.colsc[:, 6:7],
                                                    func=AF.Sqrt), deps=[i2d])
                    ysc = s(lambda e, ysl=ysl: e.activation(
                        out=ysl, in_=ysl, func=AF.Identity, scale=self.colsc[:, 7:8]),
                        deps=[iscl, zq])
                    ydep = [ysc]
                    prev["huse"] = [mmlo, mmhi]
                bufdeps[w3b] = [l3ev[m] for m in range(NM)] + last_xn_mm
                bufdeps[w2b] = [s2e1, s2e2, mmlo, mmhi]
                prev["bc"] = [ysc, l1ev[NM - 1]]
                prev["bc2"] = [ysc, l1ev[NM - 1]]
                prev["w2t"] = [mmlo, mmhi]
                prev["w2tc"] = [mmlo, mmhi]
            for jj in range(QT // 128):
                row0 = q0 + jj * 128
                ysl = yv[:, jj, :]
                gsl = s(lambda e, ysl=ysl: e.activation(
                    out=self.xsq[:, :], in_=ysl, func=AF.Silu),
                    deps=ydep + prev.get("og", []))
                dx = dma("fx", self.xt[:, :], self.x_hbm[row0:row0 + 128, :],
                         deps=prev.get("fx", []))
                ds = dma("fs", self.stg[:, 0:1024],
                         self.s_sc[row0:row0 + 128, :],
                         deps=self.s_sc_ops + prev.get("fs", []))
                c1 = v(lambda e: e.tensor_tensor(self.xsq[:, :], self.stg[:, 0:1024],
                                                 self.xsq[:, :], ALU.mult),
                       deps=[gsl, ds])
                c2 = v(lambda e: e.tensor_tensor(self.xsq[:, :], self.xt[:, :],
                                                 self.xsq[:, :], ALU.add),
                       deps=[dx])
                do = dma("fo", self.out_hbm[row0:row0 + 128, :], self.xsq[:, :],
                         deps=[c2])
                prev["og"] = [c2, do]
                prev["fx"] = [c2]
                prev["fs"] = [c1]
                out_ops.append(do)
            prev["y"] = [gsl]
            prev["xnq"] = last_xn_mm + [gsl]
        self.last_dma_out += out_ops
        self.dbg_add("yq", [128, (QT // 128) * D], lambda: self.y_q[:, :],
                     deps=ydep)



NE = 16
D = 1024
F = 64


def _padz(a, n, axis):
    pad = [(0, 0)] * a.ndim
    pad[axis] = (0, n - a.shape[axis])
    return np.pad(a, pad)


def make_core_inputs(inp, b):
    f = np.float32
    We1, We3, We2 = inp["L0_ffn_We1"], inp["L0_ffn_We3"], inp["L0_ffn_We2"]
    Ws1, Ws3, Ws2 = inp["L0_ffn_Ws1"], inp["L0_ffn_Ws3"], inp["L0_ffn_Ws2"]
    be1, be3, be2 = inp["L0_ffn_be1"], inp["L0_ffn_be3"], inp["L0_ffn_be2"]
    bs1, bs3, bs2 = inp["L0_ffn_bs1"], inp["L0_ffn_bs3"], inp["L0_ffn_bs2"]
    d = {
        "x": np.ascontiguousarray(inp["L0_attn_x"][b], f),
        "wkT": np.ascontiguousarray(inp["L0_attn_Wk"].T, f),
        "wqT": np.ascontiguousarray(inp["L0_attn_Wq"].T, f),
        "wvT": np.ascontiguousarray(inp["L0_attn_Wv"].T, f),
        "woT": np.ascontiguousarray(inp["L0_attn_Wout"].T, f),
        "nw": np.ascontiguousarray(
            np.asarray(inp["L0_attn_norm_weight"], f).reshape(8, 128).T),
        "eta": np.ascontiguousarray(np.asarray(inp["L0_attn_eta"], f).reshape(F, 1)),
        "ident_in": np.eye(128, dtype=f),
        "gwT": np.ascontiguousarray(np.asarray(inp["L0_ffn_gate_w"], f).T),
        "we1T": np.ascontiguousarray(_padz(We1, 512, 1).transpose(0, 2, 1), f),
        "we3T": np.ascontiguousarray(_padz(We3, 512, 1).transpose(0, 2, 1), f),
        "we2T": np.ascontiguousarray(
            _padz(We2[:, :, 1:].transpose(0, 2, 1), 512, 1), f),
        "w2ti": np.ascontiguousarray(We2[:, :, 0].reshape(NE, 1, 1023), f),
        "be1": np.ascontiguousarray(
            _padz(be1, 512, 1).reshape(NE, 4, 128).transpose(0, 2, 1), f),
        "be3": np.ascontiguousarray(
            _padz(be3, 512, 1).reshape(NE, 4, 128).transpose(0, 2, 1), f),
        "be2": np.ascontiguousarray(be2.reshape(NE, 1, 1023), f),
        "ws1T": np.ascontiguousarray(_padz(Ws1, 1024, 0).T, f),
        "ws3T": np.ascontiguousarray(_padz(Ws3, 1024, 0).T, f),
        "ws2T": np.ascontiguousarray(_padz(Ws2[:, 1:].T, 1024, 0), f),
        "ws2ti": np.ascontiguousarray(Ws2[:, 0].reshape(1, 1023), f),
        "bs1": np.ascontiguousarray(_padz(bs1, 1024, 0).reshape(8, 128).T, f),
        "bs3": np.ascontiguousarray(_padz(bs3, 1024, 0).reshape(8, 128).T, f),
        "bs2": np.ascontiguousarray(bs2.reshape(1, 1023), f),
    }
    return d


def rand_inputs(T=4096, B=1, seed=0):
    rng = np.random.default_rng(seed)
    f = np.float32
    r = lambda *s: rng.standard_normal(s).astype(f)
    return {
        "L0_attn_x": r(B, T, D),
        "L0_attn_norm_weight": (r(D) * 0.5 + 1.0).astype(f),
        "L0_attn_Wk": r(F, D) * 0.02,
        "L0_attn_Wv": r(D, D) * 0.02,
        "L0_attn_Wq": r(F, D) * 0.02,
        "L0_attn_eta": r(F),
        "L0_attn_Wout": r(D, D) * 0.02,
        "L0_ffn_gate_w": r(NE, D) * 0.02,
        "L0_ffn_We1": r(NE, 511, D) * 0.02,
        "L0_ffn_be1": r(NE, 511) * 0.1,
        "L0_ffn_We3": r(NE, 511, D) * 0.02,
        "L0_ffn_be3": r(NE, 511) * 0.1,
        "L0_ffn_We2": r(NE, 1023, 512) * 0.02,
        "L0_ffn_be2": r(NE, 1023) * 0.1,
        "L0_ffn_Ws1": r(1023, D) * 0.02,
        "L0_ffn_bs1": r(1023) * 0.1,
        "L0_ffn_Ws3": r(1023, D) * 0.02,
        "L0_ffn_bs3": r(1023) * 0.1,
        "L0_ffn_Ws2": r(1023, 1024) * 0.02,
        "L0_ffn_bs2": r(1023) * 0.1,
    }


def kernel(**inputs):
    from concourse.bass_utils import run_bass_kernel_spmd
    T = 4096
    B = 8
    b = KB(T=T)
    nc = b.build()
    in_maps = [make_core_inputs(inputs, i) for i in range(B)]
    res = run_bass_kernel_spmd(nc, in_maps, core_ids=list(range(B)))
    out = np.stack([np.asarray(res.results[i]["out"], np.float32)
                    for i in range(B)], axis=0)
    return out


# version 03 (final)
# speedup vs baseline: 1.1421x; 1.1421x over previous
import numpy as np
import concourse.bass as bass
import concourse.mybir as mybir

F32 = mybir.dt.float32
BF16 = mybir.dt.bfloat16
AF = mybir.ActivationFunctionType
ALU = mybir.AluOpType
AX = mybir.AxisListType

D = 1024
F = 64
NE = 16
EPS_LN = 1e-5
EPS_RMS = 1e-6
ETA_MIN = 1e-6
FD = float(F * D)
ENGINES = ("sync", "vector", "scalar", "tensor", "gpsimd")


class Op:
    __slots__ = ("eng", "fn", "deps", "grp", "seq", "inc_id", "_grp_snapshot",
                 "nf", "acc")
    def __init__(self, eng, fn, deps, grp, seq, nf=False, acc=False):
        self.eng, self.fn, self.deps, self.grp, self.seq = eng, fn, list(deps), grp, seq
        self.inc_id = None
        self._grp_snapshot = None
        self.nf = nf
        self.acc = acc


class Sched:

    def __init__(self):
        self.ops = []
        self.grp_count = {}

    def add(self, eng, fn, deps=(), grp=None, nf=False, acc=False):
        deps = [d for d in deps if d is not None]
        for d in deps:
            assert d.seq < len(self.ops)
        op = Op(eng, fn, deps, grp, len(self.ops), nf=nf, acc=acc)
        if grp is not None:
            self.grp_count[grp] = self.grp_count.get(grp, 0) + 1
            op.inc_id = self.grp_count[grp]
        self.ops.append(op)
        return op

    @staticmethod
    def _fence_needed(d, op):
        if d.grp is not None:
            return True
        if d.eng != op.eng:
            return True
        if op.grp is not None:
            return True
        if d.eng == "vector":
            return True
        if d.acc:
            return True
        return False

    def emit(self, nc):
        last = {}
        for op in self.ops:
            if op.grp is not None:
                continue
            prev = last.get(op.eng)
            if prev is not None:
                if op.eng == "vector" and not op.nf and prev not in op.deps:
                    op.deps.append(prev)
                elif op.eng == "scalar" and prev.acc and prev not in op.deps:
                    op.deps.append(prev)
            last[op.eng] = op
        needed = set()
        for op in self.ops:
            for d in op.deps:
                if d.grp is None and not self._fence_needed(d, op):
                    continue
                needed.add(d)
        counts = {e: 0 for e in ENGINES}
        for op in self.ops:
            if op.grp is None and op in needed:
                counts[op.eng] += 1
                op.inc_id = counts[op.eng]
        grp_run = {g: 0 for g in self.grp_count}
        for op in self.ops:
            op._grp_snapshot = dict(grp_run) if any(
                d.grp is not None for d in op.deps) else None
            if op.grp is not None:
                grp_run[op.grp] += 1
        sem_objs, ctxs, gsems = {}, [], {}
        for e in ENGINES:
            c = nc.semaphore("sd_" + e)
            ctxs.append(c)
            sem_objs[e] = c.__enter__()
        for g in self.grp_count:
            c = nc.semaphore("gd_" + g)
            ctxs.append(c)
            gsems[g] = c.__enter__()
        blk = nc.Block()
        block = blk.__enter__()

        def make(engname):
            def _(eng):
                waited_e = {e: 0 for e in ENGINES}
                waited_g = {g: 0 for g in self.grp_count}
                for op in self.ops:
                    if op.eng != engname:
                        continue
                    for d in op.deps:
                        if d.grp is not None:
                            vv = 16 * op._grp_snapshot[d.grp]
                            if vv > waited_g[d.grp]:
                                eng.wait_ge(gsems[d.grp], vv)
                                waited_g[d.grp] = vv
                        else:
                            if d.inc_id is None:
                                continue
                            if not self._fence_needed(d, op):
                                continue
                            if d.inc_id > waited_e[d.eng]:
                                eng.wait_ge(sem_objs[d.eng], d.inc_id)
                                waited_e[d.eng] = d.inc_id
                    inst = op.fn(eng)
                    if inst is not None:
                        try:
                            inst.annotate("op%d" % op.seq)
                        except Exception:
                            pass
                    if op.grp is not None:
                        assert inst is not None
                        inst.then_inc(gsems[op.grp], 16)
                    elif op.inc_id is not None:
                        if inst is None:
                            assert engname not in ("vector",), \
                                "vector op returned no instr; fence unsound"
                            eng.sem_inc(sem_objs[engname], 1)
                        else:
                            inst.then_inc(sem_objs[engname], 1)
            return _
        block.sync(make("sync"))
        block.vector(make("vector"))
        block.scalar(make("scalar"))
        block.tensor(make("tensor"))
        block.gpsimd(make("gpsimd"))
        blk.__exit__(None, None, None)
        for c in reversed(ctxs):
            c.__exit__(None, None, None)


class KB:
    def __init__(self, T=4096, C=64, debug=(), with_chain=True, with_post=True,
                 with_ffn=True, with_gate=None):
        self.T, self.C = T, C
        self.NCH = T // C
        self.PB = min(512, T)
        self.NPB = T // self.PB
        self.CPP = self.PB // C
        self.NT = T // 128
        self.debug = set(debug)
        self.with_chain, self.with_post, self.with_ffn = with_chain, with_post, with_ffn
        self.with_gate = with_ffn if with_gate is None else with_gate
        self.QT = min(512, T)
        self.NQ = T // self.QT
        self.sch = Sched()
        self.dbg = {}
        self.last_dma_out = []

    def v(self, fn, deps=(), nf=False):
        return self.sch.add("vector", fn, deps, nf=nf)

    def s(self, fn, deps=(), acc=False):
        return self.sch.add("scalar", fn, deps, acc=acc)

    def t(self, fn, deps=()):
        return self.sch.add("tensor", fn, deps)

    def g(self, fn, deps=()):
        return self.sch.add("gpsimd", fn, deps)

    def dma(self, grp, out, in_, deps=(), eng="sync"):
        return self.sch.add(eng, lambda e: e.dma_start(out=out, in_=in_),
                            deps, grp=grp)

    def dmat(self, grp, out, in_, deps=(), eng="sync"):
        return self.sch.add(eng, lambda e: e.dma_start_transpose(out=out, in_=in_),
                            deps, grp=grp)

    def dbg_add(self, name, shape, src_fn, deps, d=F32):
        if name not in self.debug:
            return
        t = self.nc.dram_tensor("dbg_" + name, shape, d, kind="ExternalOutput").ap()
        self.dbg[name] = (t, src_fn, deps)

    def flush_dbg(self):
        for name, (t, src_fn, deps) in self.dbg.items():
            self.last_dma_out.append(
                self.dma("dbgout", t[:, :] if len(t.shape) == 2 else t[:, :, :],
                         src_fn(), deps=deps))
        self.dbg = {}

    def build(self):
        nc = bass.Bass("TRN2", target_bir_lowering=False, debug=False,
                       detect_race_conditions=False)
        self.nc = nc
        T, C, NT = self.T, self.C, self.NT
        PB = self.PB

        dt = lambda n, s, k="ExternalInput", d=F32: nc.dram_tensor(n, s, d, kind=k).ap()
        self.x_hbm = dt("x", [T, D])
        self.wkT_h = dt("wkT", [D, F])
        self.wqT_h = dt("wqT", [D, F])
        self.wvT_h = dt("wvT", [D, D])
        self.woT_h = dt("woT", [D, D])
        self.nw_h = dt("nw", [128, 8])
        self.eta_h = dt("eta", [F, 1])
        self.ident_h = dt("ident_in", [128, 128])
        self.gwT_h = dt("gwT", [D, NE])
        self.we1T_h = dt("we1T", [NE, D, 512])
        self.we3T_h = dt("we3T", [NE, D, 512])
        self.we2T_h = dt("we2T", [NE, 512, 1023])
        self.w2ti_h = dt("w2ti", [NE, 1, 1023])
        self.be1_h = dt("be1", [NE, 128, 4])
        self.be3_h = dt("be3", [NE, 128, 4])
        self.be2_h = dt("be2", [NE, 1, 1023])
        self.ws1T_h = dt("ws1T", [D, 1024])
        self.ws3T_h = dt("ws3T", [D, 1024])
        self.ws2T_h = dt("ws2T", [1024, 1023])
        self.ws2ti_h = dt("ws2ti", [1, 1023])
        self.bs1_h = dt("bs1", [128, 8])
        self.bs3_h = dt("bs3", [128, 8])
        self.bs2_h = dt("bs2", [1, 1023])
        self.out_hbm = dt("out", [T, D], k="ExternalOutput")
        self.xn_sc = dt("xn_sc", [8, 128, T], k="Internal", d=BF16)
        self.xnr_sc = dt("xnr_sc", [8, 128, T], k="Internal", d=BF16)
        self.s_sc = dt("s_sc", [T, D], k="Internal", d=BF16)
        self.wffn_sc = dt("wffn_sc", [NE, 128, 2 * 4096 + 4 * 1023], k="Internal",
                          d=BF16)
        self.wsh_sc = dt("wsh_sc", [128, 2 * 8192 + 8 * 1023], k="Internal", d=BF16)
        PIECE = 8192
        self.PIECE = PIECE

        sb = lambda n, s, d=F32: nc.alloc_sbuf_tensor(n, s, d).ap()
        psa = lambda n, s, d=F32: nc.alloc_psum_tensor(n, s, d).ap()
        self.sb = sb

        self.wv_sb = sb("wv_sb", [128, 8 * D], BF16)
        self.wk_sb = sb("wk_sb", [128, 8 * F], BF16)
        self.wq_sb = sb("wq_sb", [128, 8 * F], BF16)
        self.gw_sb = sb("gw_sb", [128, 8 * NE], BF16)
        self.w16 = sb("w16", [128, NT * NE])
        self.ident = sb("ident", [128, 128], BF16)
        self.ident_f = sb("ident_f", [128, 128])
        self.ones_r = sb("ones_r", [1, 128])
        self.ones_cf = sb("ones_cf", [128, 1])
        self.ones_cb = sb("ones_cb", [128, 1], BF16)
        self.idcell = sb("idcell", [1, 1])
        self.etac = sb("etac", [F, 1])
        self.wcol = sb("wcol", [128, 8])
        self.xt = sb("xt", [128, D])
        self.xsq = sb("xsq", [128, D])
        self.xnt = sb("xnt", [128, D], BF16)
        self.xrt = sb("xrt", [128, D], BF16)
        self.stt_ = [sb("stt0", [128, 8]), sb("stt1", [128, 8])]
        self.wtmp = sb("wtmp", [128, 1024])
        self.wstg = sb("wstg", [128, 1024], BF16)
        self.stg = sb("stg", [128, 8 * 128], BF16)
        self.xnr_pb = sb("xnr_pb", [128, 8 * PB], BF16)
        self.v_pb = sb("v_pb", [128, 8 * PB], BF16)
        self.k_pb = sb("k_pb", [F, PB])
        self.ke_pb = sb("ke_pb", [F, PB])
        self.kb_pb = sb("kb_pb", [F, PB], BF16)
        self.keb_pb = sb("keb_pb", [F, PB], BF16)
        self.pscr = sb("pscr", [F, PB])
        C_ = C
        self.evac64 = sb("evac64", [C_, 4 * C_], BF16)
        self.a_bf = sb("a_bf", [128, 8 * C_], BF16)
        self.sq_sb = sb("sq_sb", [128, 8 * C_])
        self.sq2_sb = sb("sq2_sb", [128, 8 * C_])
        self.vtok = sb("vtok", [C_, D], BF16)
        self.kph = sb("kph", [F, C_], BF16)
        self.kphT = sb("kphT", [C_, F], BF16)
        self.m_prev = sb("m_prev", [F, D])
        self.m_bf = sb("m_bf", [F, D], BF16)
        self.uebc = sb("uebc", [F, 2])
        self.sq64 = sb("sq64", [F, C_])
        self.htf = sb("htf", [1, C_ * C_], BF16)
        self.gvtf = sb("gvtf", [1, C_ * C_], BF16)
        self.avttf = sb("avttf", [1, C_ * C_], BF16)
        self.gktf = sb("gktf", [1, C_ * C_], BF16)
        self.q6r = sb("q6r", [1, 6 * C_])
        self.bkr = sb("bkr", [1, 2 * C_])
        self.kapr = sb("kapr", [1, C_ + 1])
        self.aamr = sb("aamr", [1, C_])
        self.mhatr = sb("mhatr", [1, C_])
        self.rowsc = sb("rowsc", [1, 8 * C_])
        self.rows2 = sb("rows2", [1, 8 * C_])
        self.phr = sb("phr", [1, C_])
        self.ghr = sb("ghr", [1, C_])
        self.gher = sb("gher", [1, C_])
        self.q1r = sb("q1r", [1, C_])
        self.q2r = sb("q2r", [1, C_])
        self.scr = sb("scr", [1, C_])
        self.scr6 = sb("scr6", [1, 8])
        self.gcol = sb("gcol", [C_, 1], BF16)
        self.st = sb("st", [1, 32])
        QT = self.QT
        self.xnsl = sb("xnsl", [128, 8 * QT], BF16)
        self.y_q = sb("y_q", [128, (QT // 128) * D])
        self.wff = sb("wff", [128, 2 * PIECE], BF16)
        self.h_sb = sb("h_sb", [128, 8 * 512], BF16)
        self.ab_sb = sb("ab_sb", [128, 4 * 512], BF16)
        self.ht_row = sb("ht_row", [1, 512], BF16)
        self.s2_sb = sb("s2_sb", [128, 1024])
        self.sqf = sb("sqf", [128, 512])
        self.becol = sb("becol", [128, 8])
        self.bshx = sb("bshx", [128, 8])
        self.be2row = sb("be2row", [1, 1024])
        self.w2time = sb("w2time", [1, 1024], BF16)
        self.colsc = sb("colsc", [128, 8])
        self.sc_t = sb("sc_t", [128, NE])
        self.eq_t = sb("eq_t", [128, NE])
        self.mk_t = sb("mk_t", [128, NE])
        self.mcol = sb("mcol", [128, 4])
        self.p_a = psa("p_a", [128, 512])
        self.p_b = psa("p_b", [128, 512])
        self.p_c = psa("p_c", [128, 512])
        self.p_d = psa("p_d", [128, 512])
        self.p_e = psa("p_e", [128, 512])
        self.p_f = psa("p_f", [128, 512])
        self.p_g = psa("p_g", [128, 512])
        self.p_tp = psa("p_tp", [128, 256], BF16)

        self.p0_consts()
        self.p1_ln()
        self.p2_weights()
        if self.with_gate:
            self.p3_gate()
        if self.with_chain:
            self.p4_chain()
        if self.with_post:
            self.p5_post()
        if self.with_ffn:
            self.p6_ffn()
        self.flush_dbg()
        self.g(lambda e: e.engine_nop(), deps=self.last_dma_out)
        trunc = getattr(self, "_trunc", None)
        if trunc is not None:
            keep = self.sch.ops[:trunc]
            self.sch.ops = keep
        self.sch.emit(nc)
        return nc

    def p0_consts(self):
        v, s, g, dma = self.v, self.s, self.g, self.dma
        d_id = dma("consts", self.ident_f[:, :], self.ident_h[:, :])
        o1 = g(lambda e: e.memset(self.ones_r[:, :], 1.0))
        o2 = g(lambda e: e.memset(self.ones_cf[:, :], 1.0))
        o3 = g(lambda e: e.memset(self.ones_cb[:, :], 1.0))
        o4 = g(lambda e: e.memset(self.idcell[:, :], 1.0))
        self.op_ones = [o1, o2, o3, o4]
        self.op_nw = dma("consts", self.wcol[:, :], self.nw_h[:, :])
        d_eta = dma("consts", self.etac[:, :], self.eta_h[:, :])
        a1 = s(lambda e: e.activation(out=self.etac[:, :], in_=self.etac[:, :],
                                      func=AF.Abs), deps=[d_eta])
        self.op_eta = v(lambda e: e.tensor_scalar_max(self.etac[:, :],
                                                      self.etac[:, :], ETA_MIN),
                        deps=[a1])
        self.op_identb = v(lambda e: e.tensor_copy(self.ident[:, :],
                                                   self.ident_f[:, :]),
                           deps=[d_id])

    def p1_ln(self):
        v, s, dma, dmat = self.v, self.s, self.dma, self.dmat
        T, NT = self.T, self.NT
        pu = {}
        self.xn_ts = []
        self.xr_ts = []
        for i in range(NT):
            ib = i % 2
            st_ = self.stt_[ib]
            dx = dma("gx", self.xt[:, :], self.x_hbm[i * 128:(i + 1) * 128, :],
                     deps=pu.get("xt", []))
            a_sq = s(lambda e, st_=st_: e.activation(
                out=self.xsq[:, :], in_=self.xt[:, :], func=AF.Square,
                accum_out=st_[:, 1:2]),
                deps=[dx] + pu.get("xsq", []) + pu.get(("st", ib), []), acc=True)
            r_sum = v(lambda e, st_=st_: e.reduce_sum(
                st_[:, 0:1], self.xt[:, :], axis=AX.X),
                deps=[dx] + pu.get(("st", ib), []))
            v(lambda e, st_=st_: e.tensor_scalar_mul(st_[:, 0:1], st_[:, 0:1], 1.0 / D))
            v(lambda e, st_=st_: e.tensor_scalar_mul(st_[:, 1:2], st_[:, 1:2], 1.0 / D),
              deps=[a_sq])
            v(lambda e, st_=st_: e.tensor_scalar(st_[:, 2:3], st_[:, 0:1],
                                                 st_[:, 0:1], None, ALU.mult))
            v(lambda e, st_=st_: e.tensor_sub(st_[:, 2:3], st_[:, 1:2], st_[:, 2:3]))
            v(lambda e, st_=st_: e.tensor_scalar_add(st_[:, 3:4], st_[:, 2:3], EPS_LN))
            rc = v(lambda e, st_=st_: e.reciprocal(st_[:, 4:5], st_[:, 3:4]))
            sq1 = s(lambda e, st_=st_: e.activation(out=st_[:, 5:6], in_=st_[:, 4:5],
                                                    func=AF.Sqrt), deps=[rc])
            nb = v(lambda e, st_=st_: e.tensor_scalar(
                st_[:, 6:7], st_[:, 0:1], st_[:, 5:6], -1.0, ALU.mult, ALU.mult),
                deps=[sq1])
            xn_op = s(lambda e, st_=st_: e.activation(
                out=self.xnt[:, :], in_=self.xt[:, :], func=AF.Identity,
                scale=st_[:, 5:6], bias=st_[:, 6:7]),
                deps=[nb] + pu.get("xnt", []))
            v(lambda e, st_=st_: e.tensor_tensor(
                st_[:, 7:8], st_[:, 2:3], st_[:, 4:5], ALU.mult))
            v(lambda e, st_=st_: e.tensor_scalar_add(st_[:, 7:8], st_[:, 7:8], EPS_RMS))
            rc2 = v(lambda e, st_=st_: e.reciprocal(st_[:, 7:8], st_[:, 7:8]))
            sq2 = s(lambda e, st_=st_: e.activation(out=st_[:, 7:8], in_=st_[:, 7:8],
                                                    func=AF.Sqrt), deps=[rc2])
            rs = v(lambda e, st_=st_: e.tensor_tensor(
                st_[:, 5:6], st_[:, 5:6], st_[:, 7:8], ALU.mult), deps=[sq2])
            nb2 = v(lambda e, st_=st_: e.tensor_tensor(
                st_[:, 6:7], st_[:, 6:7], st_[:, 7:8], ALU.mult))
            xr_op = s(lambda e, st_=st_: e.activation(
                out=self.xrt[:, :], in_=self.xt[:, :], func=AF.Identity,
                scale=st_[:, 5:6], bias=st_[:, 6:7]),
                deps=[nb2, rs] + pu.get("xrt", []))
            tn, sn = [], []
            for c in range(8):
                tp = dmat("tpn", self.stg[:, c * 128:(c + 1) * 128],
                          self.xnt[:, c * 128:(c + 1) * 128],
                          deps=[xn_op] + pu.get("stgn", []))
                tn.append(tp)
                sn.append(dma("tsn", self.xn_sc[c, :, i * 128:(i + 1) * 128],
                              self.stg[:, c * 128:(c + 1) * 128], deps=[tp]))
            tr, sr = [], []
            for c in range(8):
                tp = dmat("tpr", self.stg[:, c * 128:(c + 1) * 128],
                          self.xrt[:, c * 128:(c + 1) * 128],
                          deps=[xr_op, sn[c]])
                tr.append(tp)
                sr.append(dma("tsr", self.xnr_sc[c, :, i * 128:(i + 1) * 128],
                              self.stg[:, c * 128:(c + 1) * 128], deps=[tp]))
            self.xn_ts.append(sn)
            self.xr_ts.append(sr)
            pu["xt"] = [a_sq, r_sum, xn_op, xr_op]
            pu["xsq"] = [a_sq]
            pu[("st", ib)] = [xn_op, xr_op, nb2, rs]
            pu["xnt"] = tn
            pu["xrt"] = tr
            pu["stgn"] = sr

    def p2_weights(self):
        v, g, dma = self.v, self.g, self.dma
        wp = {}
        fold = [self.op_nw]

        def prep(dst, src_h, cols, foldw, extra_deps=()):
            lasts = []
            for c in range(8):
                dl = dma("wp", self.wtmp[:, 0:cols],
                         src_h[c * 128:(c + 1) * 128, :],
                         deps=wp.get("b", []) + list(extra_deps))
                if foldw:
                    cv = v(lambda e, c=c, cols=cols, dst=dst: e.tensor_scalar(
                        dst[:, c * cols:(c + 1) * cols], self.wtmp[:, 0:cols],
                        self.wcol[:, c:c + 1], None, ALU.mult), deps=[dl] + fold)
                else:
                    cv = v(lambda e, c=c, cols=cols, dst=dst: e.tensor_copy(
                        dst[:, c * cols:(c + 1) * cols], self.wtmp[:, 0:cols]),
                        deps=[dl])
                wp["b"] = [cv]
                lasts.append(cv)
            return lasts
        self.prep_w = prep
        self.wv_done = prep(self.wv_sb, self.wvT_h, D, True)
        self.wk_done = prep(self.wk_sb, self.wkT_h, F, True)
        self.wq_done = prep(self.wq_sb, self.wqT_h, F, True)
        self.gw_done = prep(self.gw_sb, self.gwT_h, NE, False)
        if not self.with_ffn:
            self.wffn_done = []
            return
        lastg = []
        for e_ix in range(NE):
            for src, ncols, nch, off in (
                    (self.we1T_h[e_ix], 512, 8, 0),
                    (self.we3T_h[e_ix], 512, 8, 4096),
                    (self.we2T_h[e_ix], 1023, 4, 8192)):
                for c in range(nch):
                    dl = dma("wp", self.wtmp[:, 0:ncols], src[c * 128:(c + 1) * 128, :],
                             deps=wp.get("b", []))
                    cast = g(lambda e, ncols=ncols: e.tensor_copy(
                        self.wstg[:, 0:ncols], self.wtmp[:, 0:ncols]),
                        deps=[dl] + lastg)
                    sd = dma("wfs", self.wffn_sc[e_ix, :, off + c * ncols:
                                                 off + (c + 1) * ncols],
                             self.wstg[:, 0:ncols], deps=[cast])
                    wp["b"] = [cast]
                    lastg = [sd]
        for src, ncols, nch, off in (
                (self.ws1T_h, 1024, 8, 0),
                (self.ws3T_h, 1024, 8, 8192),
                (self.ws2T_h, 1023, 8, 16384)):
            for c in range(nch):
                dl = dma("wp", self.wtmp[:, 0:ncols], src[c * 128:(c + 1) * 128, :],
                         deps=wp.get("b", []))
                cast = g(lambda e, ncols=ncols: e.tensor_copy(
                    self.wstg[:, 0:ncols], self.wtmp[:, 0:ncols]),
                    deps=[dl] + lastg)
                sd = dma("wfs", self.wsh_sc[:, off + c * ncols: off + (c + 1) * ncols],
                         self.wstg[:, 0:ncols], deps=[cast])
                wp["b"] = [cast]
                lastg = [sd]
        self.wffn_done = lastg

    def p3_gate(self):
        v, s, t, dma = self.v, self.s, self.t, self.dma
        NT = self.NT
        p = self.p_g
        prev = []
        mm_prev = []
        for i in range(NT):
            xnd = []
            for c in range(8):
                xnd.append(dma("gxn", self.stg[:, c * 128:(c + 1) * 128],
                               self.xn_sc[c, :, i * 128:(i + 1) * 128],
                               deps=[self.xn_ts[i][c]] + mm_prev))
            mms = []
            for c in range(8):
                mms.append(t(lambda e, c=c: e.matmul(
                    p[0:128, 0:NE], self.stg[:, c * 128:(c + 1) * 128],
                    self.gw_sb[:, c * NE:(c + 1) * NE],
                    start=(c == 0), stop=(c == 7)),
                    deps=[xnd[c]] + self.gw_done + prev))
            m1 = v(lambda e: e.reduce_max(self.mcol[:, 0:1], p[0:128, 0:NE],
                                          axis=AX.X), deps=[mms[-1]])
            nm = v(lambda e: e.tensor_scalar_mul(self.mcol[:, 0:1],
                                                 self.mcol[:, 0:1], -1.0))
            ex = s(lambda e: e.activation(out=self.sc_t[:, :], in_=p[0:128, 0:NE],
                                          func=AF.Exp, bias=self.mcol[:, 0:1]),
                   deps=[nm] + prev)
            sm = v(lambda e: e.reduce_sum(self.mcol[:, 1:2], self.sc_t[:, :],
                                          axis=AX.X), deps=[ex])
            v(lambda e: e.reciprocal(self.mcol[:, 1:2], self.mcol[:, 1:2]))
            sc = v(lambda e: e.tensor_scalar(self.sc_t[:, :], self.sc_t[:, :],
                                             self.mcol[:, 1:2], None, ALU.mult))
            m1b = v(lambda e: e.reduce_max(self.mcol[:, 2:3], self.sc_t[:, :],
                                           axis=AX.X))
            eq = v(lambda e: e.tensor_scalar(self.eq_t[:, :], self.sc_t[:, :],
                                             self.mcol[:, 2:3], None, ALU.is_equal))
            mk = v(lambda e: e.scalar_tensor_tensor(self.mk_t[:, :], self.eq_t[:, :],
                                                    -1e30, self.sc_t[:, :],
                                                    ALU.mult, ALU.add))
            m2 = v(lambda e: e.reduce_max(self.mcol[:, 3:4], self.mk_t[:, :],
                                          axis=AX.X))
            hit = v(lambda e: e.tensor_scalar(self.eq_t[:, :], self.sc_t[:, :],
                                              self.mcol[:, 3:4], None, ALU.is_ge))
            w = v(lambda e, i=i: e.tensor_tensor(self.w16[:, i * NE:(i + 1) * NE],
                                                 self.sc_t[:, :], self.eq_t[:, :],
                                                 ALU.mult))
            prev = [w, sc, eq, mk, hit, m1, sm, m1b, m2, ex]
            mm_prev = mms
        self.gate_done = prev
        self.dbg_add("w16", [128, self.NT * NE], lambda: self.w16[:, :], deps=prev)

    def p4_chain(self):
        v, s, t, g, dma = self.v, self.s, self.t, self.g, self.dma
        T, C, PB, NPB, CPP = self.T, self.C, self.PB, self.NPB, self.CPP
        st = self.st
        cell = lambda i: st[0:1, i:i + 1]
        init0 = [
            g(lambda e: e.memset(self.m_prev[:, :], 0.0)),
            g(lambda e: e.memset(self.m_bf[:, :], 0.0)),
            g(lambda e: e.memset(st[:, :], 0.0)),
            g(lambda e: e.memset(self.kapr[0:1, C:C + 1], 0.0)),
        ]
        init1 = [g(lambda e: e.memset(cell(3), 1.0), deps=init0)]
        self.chain_mbf_dep = init0 + init1
        prev_tail = init0 + init1 + getattr(self, "gate_done", [])
        vpb_done = {}
        for pb in range(NPB):
            tiles = range(pb * (PB // 128), (pb + 1) * (PB // 128))
            sc_deps = [self.xr_ts[i][c] for i in tiles for c in range(8)]
            xnr_d = []
            for c in range(8):
                xnr_d.append(dma("xpb", self.xnr_pb[:, c * PB:(c + 1) * PB],
                                 self.xnr_sc[c, :, pb * PB:(pb + 1) * PB],
                                 deps=sc_deps + vpb_done.get("xnr", [])))
            vops = []
            for ec in range(8):
                pban = self.p_a if ec % 2 == 0 else self.p_b
                mm = None
                for dc in range(8):
                    mm = t(lambda e, ec=ec, dc=dc, pban=pban: e.matmul(
                        pban[0:128, 0:PB],
                        self.wv_sb[:, dc * D + ec * 128: dc * D + (ec + 1) * 128],
                        self.xnr_pb[:, dc * PB:(dc + 1) * PB],
                        start=(dc == 0), stop=(dc == 7)),
                        deps=([xnr_d[dc]] + self.wv_done + vpb_done.get(ec % 2, [])
                              + prev_tail))
                ev = v(lambda e, ec=ec, pban=pban: e.tensor_copy(
                    self.v_pb[:, ec * PB:(ec + 1) * PB], pban[0:128, 0:PB]),
                    deps=[mm])
                vpb_done[ec % 2] = [ev]
                vops.append(ev)
            kmm = None
            for dc in range(8):
                kmm = t(lambda e, dc=dc: e.matmul(
                    self.p_c[0:F, 0:PB], self.wk_sb[:, dc * F:(dc + 1) * F],
                    self.xnr_pb[:, dc * PB:(dc + 1) * PB],
                    start=(dc == 0), stop=(dc == 7)),
                    deps=[xnr_d[dc]] + self.wk_done + vpb_done.get("k", [])
                    + prev_tail)
            mn = v(lambda e: e.tensor_single_scalar(self.pscr[:, 0:PB],
                                                    self.p_c[0:F, 0:PB], 0.0,
                                                    ALU.min),
                   deps=[kmm] + vpb_done.get("pscr", []))
            exk = s(lambda e: e.activation(out=self.pscr[:, 0:PB],
                                           in_=self.pscr[:, 0:PB], func=AF.Exp),
                    deps=[mn])
            kf = v(lambda e: e.scalar_tensor_tensor(self.k_pb[:, 0:PB],
                                                    self.p_c[0:F, 0:PB], 0.0,
                                                    self.pscr[:, 0:PB],
                                                    ALU.max, ALU.add), deps=[exk])
            kef = v(lambda e: e.tensor_scalar(self.ke_pb[:, 0:PB], self.k_pb[:, 0:PB],
                                              self.etac[:, 0:1], None, ALU.mult),
                    deps=[self.op_eta])
            kb1 = v(lambda e: e.tensor_copy(self.kb_pb[:, 0:PB], self.k_pb[:, 0:PB]))
            kb2 = v(lambda e: e.tensor_copy(self.keb_pb[:, 0:PB], self.ke_pb[:, 0:PB]))
            kops = [kf, kef, kb1, kb2]
            vpb_done["k"] = kops
            vpb_done["pscr"] = [kf]
            vpb_done["xnr"] = [kmm] + vops
            for cc in range(CPP):
                prev_tail = self.emit_chunk(cc, vops, kops, prev_tail)
        self.chain_done = prev_tail
        self.dbg_add("mfinal", [F, D], lambda: self.m_prev[:, :], deps=prev_tail)

    def emit_chunk(self, cc, vops, kops, prev_tail):
        stage = getattr(self, "_prep_stage", 99)
        if stage == 0:
            return prev_tail
        v, s, t, dma = self.v, self.s, self.t, self.dma
        C, PB = self.C, self.PB
        st = self.st
        cell = lambda i: st[0:1, i:i + 1]
        vsl = lambda ec: self.v_pb[:, ec * PB + cc * C: ec * PB + cc * C + C]
        kbsl = self.kb_pb[:, cc * C:cc * C + C]
        kebsl = self.keb_pb[:, cc * C:cc * C + C]
        kfsl = self.k_pb[:, cc * C:cc * C + C]
        kefsl = self.ke_pb[:, cc * C:cc * C + C]
        p_gr, p_A, p_B = self.p_e, self.p_c, self.p_d
        p_rw, p_r2 = self.p_g, self.p_f

        gv_mm = None
        for dc in range(8):
            gv_mm = t(lambda e, dc=dc: e.matmul(
                p_gr[0:C, 0:C], vsl(dc), vsl(dc), start=(dc == 0), stop=(dc == 7)),
                deps=[vops[dc]] + prev_tail)
        ht_mm = t(lambda e: e.matmul(p_gr[0:C, C:2 * C], kbsl, kebsl,
                                     start=True, stop=True), deps=kops + prev_tail)
        gk_mm = t(lambda e: e.matmul(p_gr[0:C, 3 * C:4 * C], kebsl, kebsl,
                                     start=True, stop=True), deps=kops)
        if stage == 1:
            gev = [
                v(lambda e: e.tensor_copy(self.evac64[:, 0:C], p_gr[0:C, C:2 * C]),
                  deps=[ht_mm, gv_mm, gk_mm] + prev_tail)]
            return prev_tail + gev
        a_mm, b_mm = [], []
        for dc in range(8):
            a_mm.append(t(lambda e, dc=dc: e.matmul(
                p_A[0:128, dc * C:(dc + 1) * C],
                self.m_bf[:, dc * 128:(dc + 1) * 128],
                kbsl, start=True, stop=True),
                deps=kops + self.chain_mbf_dep + prev_tail))
            b_mm.append(t(lambda e, dc=dc: e.matmul(
                p_B[0:128, dc * C:(dc + 1) * C],
                self.m_bf[:, dc * 128:(dc + 1) * 128],
                kebsl, start=True, stop=True),
                deps=kops + self.chain_mbf_dep + prev_tail))
        aev = []
        for dc in range(8):
            aev.append(v(lambda e, dc=dc: e.tensor_copy(
                self.a_bf[:, dc * C:(dc + 1) * C], p_A[0:128, dc * C:(dc + 1) * C]),
                deps=[a_mm[7]] + prev_tail))
        if stage == 21:
            return prev_tail + aev
        avt_mm = None
        for dc in range(8):
            avt_mm = t(lambda e, dc=dc: e.matmul(
                p_gr[0:C, 2 * C:3 * C], self.a_bf[:, dc * C:(dc + 1) * C], vsl(dc),
                start=(dc == 0), stop=(dc == 7)), deps=[aev[dc], vops[dc]])
        if stage == 22:
            fin = v(lambda e: e.tensor_copy(self.evac64[:, 0:C], p_gr[0:C, 2 * C:3 * C]),
                    deps=[avt_mm] + prev_tail)
            return prev_tail + aev + [fin]
        sqa = []
        for dc in range(8):
            sqa.append(s(lambda e, dc=dc: e.activation(
                out=self.sq_sb[:, dc * C:(dc + 1) * C],
                in_=p_A[0:128, dc * C:(dc + 1) * C], func=AF.Square),
                deps=[a_mm[7], aev[7]] + prev_tail))
        p2m = None
        for dc in range(8):
            p2m = t(lambda e, dc=dc: e.matmul(
                p_rw[0:1, 0:C], self.ones_cf[:, 0:1],
                self.sq_sb[:, dc * C:(dc + 1) * C],
                start=(dc == 0), stop=(dc == 7)),
                deps=[sqa[dc]] + self.op_ones + prev_tail)
        if stage == 23:
            fin = v(lambda e: e.tensor_copy(self.rowsc[0:1, 0:C], p_rw[0:1, 0:C]),
                    deps=[p2m] + prev_tail)
            return prev_tail + aev + [fin]
        p3v = []
        for dc in range(8):
            p3v.append(v(lambda e, dc=dc: e.tensor_tensor(
                self.sq2_sb[:, dc * C:(dc + 1) * C],
                p_A[0:128, dc * C:(dc + 1) * C], vsl(dc), ALU.mult),
                deps=[a_mm[7], vops[dc], sqa[7]] + prev_tail))
        p3m = None
        for dc in range(8):
            p3m = t(lambda e, dc=dc: e.matmul(
                p_rw[0:1, C:2 * C], self.ones_cf[:, 0:1],
                self.sq2_sb[:, dc * C:(dc + 1) * C],
                start=(dc == 0), stop=(dc == 7)), deps=[p3v[dc]])
        p4m = None
        for dc in range(8):
            p4m = t(lambda e, dc=dc: e.matmul(
                p_rw[0:1, 2 * C:3 * C], self.ones_cb[:, 0:1],
                self.a_bf[:, dc * C:(dc + 1) * C],
                start=(dc == 0), stop=(dc == 7)), deps=[aev[dc]] + self.op_ones)
        if stage == 24:
            fin = v(lambda e: e.tensor_copy(self.rowsc[0:1, 0:3 * C], p_rw[0:1, 0:3 * C]),
                    deps=[p2m, p3m, p4m] + prev_tail)
            return prev_tail + aev + [fin]
        sqv = []
        for dc in range(8):
            sqv.append(s(lambda e, dc=dc: e.activation(
                out=self.sq_sb[:, dc * C:(dc + 1) * C], in_=vsl(dc), func=AF.Square),
                deps=[vops[dc], p2m]))
        p1m = None
        for dc in range(8):
            p1m = t(lambda e, dc=dc: e.matmul(
                p_rw[0:1, 3 * C:4 * C], self.ones_cf[:, 0:1],
                self.sq_sb[:, dc * C:(dc + 1) * C],
                start=(dc == 0), stop=(dc == 7)), deps=[sqv[dc]])
        svm = None
        for dc in range(8):
            svm = t(lambda e, dc=dc: e.matmul(
                p_rw[0:1, 4 * C:5 * C], self.ones_cb[:, 0:1], vsl(dc),
                start=(dc == 0), stop=(dc == 7)),
                deps=[vops[dc]] + self.op_ones + prev_tail)
        bvv = []
        for dc in range(8):
            bvv.append(v(lambda e, dc=dc: e.tensor_tensor(
                self.sq2_sb[:, dc * C:(dc + 1) * C],
                p_B[0:128, dc * C:(dc + 1) * C], vsl(dc), ALU.mult),
                deps=[b_mm[7], vops[dc], p3m]))
        bvm = None
        for dc in range(8):
            bvm = t(lambda e, dc=dc: e.matmul(
                p_rw[0:1, 5 * C:6 * C], self.ones_cf[:, 0:1],
                self.sq2_sb[:, dc * C:(dc + 1) * C],
                start=(dc == 0), stop=(dc == 7)), deps=[bvv[dc]])
        kapm = t(lambda e: e.matmul(p_rw[0:1, 6 * C:7 * C], self.ones_cf[0:F, 0:1],
                                    kfsl, start=True, stop=True),
                 deps=kops + self.op_ones + prev_tail)
        kapem = t(lambda e: e.matmul(p_rw[0:1, 7 * C:8 * C], self.ones_cf[0:F, 0:1],
                                     kefsl, start=True, stop=True), deps=kops)
        ketam = t(lambda e: e.matmul(p_r2[0:1, 0:C], self.etac[:, 0:1], kfsl,
                                     start=True, stop=True),
                  deps=kops + [self.op_eta] + prev_tail)
        kesq = s(lambda e: e.activation(out=self.sq64[:, 0:C], in_=kefsl,
                                        func=AF.Square), deps=kops + prev_tail)
        kesqm = t(lambda e: e.matmul(p_r2[0:1, C:2 * C], self.ones_cf[0:F, 0:1],
                                     self.sq64[:, 0:C], start=True, stop=True),
                  deps=[kesq])
        if stage == 2:
            r_ev0 = v(lambda e: e.tensor_copy(self.rowsc[0:1, :], p_rw[0:1, 0:8 * C]),
                      deps=[p2m, p3m, p4m, p1m, svm, bvm, kapm, kapem, avt_mm,
                            gv_mm, ht_mm, gk_mm, ketam, kesqm])
            return prev_tail + [r_ev0]
        r_ev = v(lambda e: e.tensor_copy(self.rowsc[0:1, :], p_rw[0:1, 0:8 * C]),
                 deps=[p2m, p3m, p4m, p1m, svm, bvm, kapm, kapem])
        r_ev2 = v(lambda e: e.tensor_copy(self.rows2[0:1, 0:2 * C],
                                          p_r2[0:1, 0:2 * C]),
                  deps=[ketam, kesqm] + prev_tail)
        RW = self.rowsc
        P2s, P3s, P4s = RW[0:1, 0:C], RW[0:1, C:2 * C], RW[0:1, 2 * C:3 * C]
        P1s, SVs, BVs = (RW[0:1, 3 * C:4 * C], RW[0:1, 4 * C:5 * C],
                         RW[0:1, 5 * C:6 * C])
        KAPs, KAPEs = RW[0:1, 6 * C:7 * C], RW[0:1, 7 * C:8 * C]
        KETAs, KESQs = self.rows2[0:1, 0:C], self.rows2[0:1, C:2 * C]
        q6v = lambda j: self.q6r[0:1, :].rearrange("o (t k) -> o t k", k=6)[:, :, j]
        bkv = lambda j: self.bkr[0:1, :].rearrange("o (t k) -> o t k", k=2)[:, :, j]
        asm = [r_ev, r_ev2]
        A = asm.append
        A(v(lambda e: e.tensor_copy(q6v(0), P1s), deps=[r_ev]))
        A(v(lambda e: e.tensor_scalar_mul(q6v(1), P3s, -2.0), deps=[r_ev]))
        A(v(lambda e: e.tensor_tensor(q6v(2), KAPs, SVs, ALU.mult), deps=[r_ev]))
        A(v(lambda e: e.tensor_scalar_mul(q6v(2), q6v(2), -2.0)))
        A(v(lambda e: e.tensor_copy(q6v(3), P2s), deps=[r_ev]))
        A(v(lambda e: e.tensor_tensor(q6v(4), KAPs, P4s, ALU.mult), deps=[r_ev]))
        A(v(lambda e: e.tensor_scalar_mul(q6v(4), q6v(4), 2.0)))
        A(v(lambda e: e.tensor_tensor(q6v(5), KAPs, KAPs, ALU.mult), deps=[r_ev]))
        A(v(lambda e: e.tensor_scalar_mul(q6v(5), q6v(5), float(D))))
        A(v(lambda e: e.tensor_scalar_mul(bkv(0), BVs, 2.0 / FD), deps=[r_ev]))
        A(v(lambda e: e.tensor_tensor(bkv(1), KAPEs, SVs, ALU.mult), deps=[r_ev]))
        A(v(lambda e: e.tensor_scalar_mul(bkv(1), bkv(1), 2.0 / FD)))
        A(v(lambda e: e.tensor_tensor(self.mhatr[0:1, :], KETAs, SVs, ALU.mult),
            deps=[r_ev, r_ev2]))
        A(v(lambda e: e.tensor_scalar_mul(self.mhatr[0:1, :], self.mhatr[0:1, :],
                                          1.0 / FD)))
        A(v(lambda e: e.tensor_tensor(self.aamr[0:1, :], KESQs, P1s, ALU.mult),
            deps=[r_ev, r_ev2]))
        A(v(lambda e: e.tensor_scalar_mul(self.aamr[0:1, :], self.aamr[0:1, :],
                                          1.0 / FD)))
        A(v(lambda e: e.tensor_tensor(self.scr[0:1, :], self.mhatr[0:1, :],
                                      self.mhatr[0:1, :], ALU.mult)))
        A(v(lambda e: e.tensor_sub(self.aamr[0:1, :], self.aamr[0:1, :],
                                   self.scr[0:1, :])))
        A(v(lambda e: e.tensor_copy(self.kapr[0:1, 0:C], KAPs), deps=[r_ev]))
        grw = [ht_mm, gv_mm, avt_mm, gk_mm]
        gev = [
            v(lambda e: e.tensor_copy(self.evac64[:, 0:C], p_gr[0:C, C:2 * C]),
              deps=grw + prev_tail),
            v(lambda e: e.tensor_copy(self.evac64[:, C:2 * C], p_gr[0:C, 0:C]),
              deps=grw + prev_tail),
            v(lambda e: e.tensor_copy(self.evac64[:, 2 * C:3 * C],
                                      p_gr[0:C, 2 * C:3 * C]),
              deps=grw + prev_tail),
            v(lambda e: e.tensor_scalar_mul(self.evac64[:, 3 * C:4 * C],
                                            p_gr[0:C, 3 * C:4 * C], 2.0 / FD),
              deps=grw + prev_tail),
        ]
        flv = lambda dst: dst[0:1, :].rearrange("o (p f) -> o p f", p=C)
        fl = [
            dma("fl", flv(self.htf), self.evac64[:, 0:C], deps=[gev[0]] + prev_tail),
            dma("fl", flv(self.gvtf), self.evac64[:, C:2 * C],
                deps=[gev[1]] + prev_tail),
            dma("fl", flv(self.avttf), self.evac64[:, 2 * C:3 * C],
                deps=[gev[2]] + prev_tail),
            dma("fl", flv(self.gktf), self.evac64[:, 3 * C:4 * C],
                deps=[gev[3]] + prev_tail),
        ]
        sinit = [
            v(lambda e: e.memset(self.phr[0:1, :], 0.0), deps=prev_tail),
            v(lambda e: e.memset(st[0:1, 4:9], 0.0), deps=prev_tail),
            v(lambda e: e.memset(st[0:1, 25:26], 0.0), deps=prev_tail),
        ]
        sinit.append(v(lambda e: e.memset(cell(4), 1.0)))
        sinit.append(v(lambda e: e.memset(cell(6), 1.0)))
        prep_done = asm + gev + fl + sinit + [kesqm]
        if stage == 3:
            return prep_done
        nsteps = getattr(self, "_nsteps", None)
        last = self.emit_steps(cc, prep_done, nsteps)
        if stage == 4:
            return last
        return self.emit_mupdate(cc, last, vops)

    def emit_steps(self, cc, prep_done, nsteps=None):
        v, s, t = self.v, self.s, self.t
        C = self.C
        st = self.st
        I_ONE, I_U, I_E = 3, 4, 5
        I_U2, I_E2 = 6, 8
        I_AL, I_LIN, I_WSQ = 10, 11, 12
        I_C2, I_CC = 13, 14
        I_YD, I_G1B, I_G12, I_T1, I_VAR, I_SIG2 = 15, 16, 17, 18, 19, 20
        I_IS2, I_ISG, I_NU, I_MU, I_EK = 21, 22, 23, 24, 25
        cell = lambda i: st[0:1, i:i + 1]
        htf, gvtf, avttf, gktf = self.htf, self.gvtf, self.avttf, self.gktf
        phr, ghr, gher = self.phr, self.ghr, self.gher
        q1r, q2r, scr, scr6 = self.q1r, self.q2r, self.scr, self.scr6
        p_ch = self.p_f
        row = lambda tf, tt: tf[0:1, tt * C:(tt + 1) * C]
        last_ops = prep_done
        pv = {}
        for tt in range(C if nsteps is None else min(C, nsteps)):
            dep0 = prep_done if tt == 0 else []
            pd = lambda k: dep0 if tt == 0 else [pv[k]]
            a1 = v(lambda e, tt=tt: e.tensor_tensor(ghr[0:1, :], phr[0:1, :],
                                                    row(htf, tt), ALU.mult),
                   deps=dep0)
            a2 = v(lambda e, tt=tt: e.tensor_tensor(gher[0:1, :], phr[0:1, :],
                                                    row(gktf, tt), ALU.mult),
                   deps=pd("c2"), nf=True)
            t1 = t(lambda e: e.matmul(p_ch[0:C, 200:201], ghr[0:1, 0:C],
                                      self.idcell[0:1, 0:1], start=True, stop=True),
                   deps=[a1] + self.op_ones + dep0)
            v(lambda e, tt=tt: e.scalar_tensor_tensor(
                q1r[0:1, :], row(avttf, tt), cell(I_U), row(gvtf, tt),
                ALU.mult, ALU.subtract), deps=pd("c3"), nf=True)
            v(lambda e: e.scalar_tensor_tensor(
                q2r[0:1, :], self.rowsc[0:1, 4 * C:5 * C], cell(I_EK), q1r[0:1, :],
                ALU.mult, ALU.add))
            a5 = v(lambda e: e.scalar_tensor_tensor(
                scr[0:1, :], q2r[0:1, :], 2.0, ghr[0:1, :], ALU.mult, ALU.mult,
                accum_out=cell(I_LIN)))
            gce = v(lambda e: e.tensor_copy(self.gcol[:, 0:1], p_ch[0:C, 200:201]),
                    deps=[t1], nf=True)
            t2 = t(lambda e: e.matmul(p_ch[0:1, 256:256 + C], self.gcol[:, 0:1],
                                      self.evac64[:, C:2 * C], start=True,
                                      stop=True),
                   deps=[gce])
            a6 = v(lambda e, tt=tt: e.scalar_tensor_tensor(
                scr6[0:1, 0:6], self.q6r[0:1, 6 * tt:6 * tt + 6], 1.0,
                st[0:1, I_ONE:I_ONE + 6], ALU.mult, ALU.mult, accum_out=cell(I_AL)),
                deps=pd("c8"), nf=True)
            a7 = v(lambda e, tt=tt: e.scalar_tensor_tensor(
                scr[0:1, :], row(gvtf, tt), 1.0, gher[0:1, :], ALU.mult, ALU.mult,
                accum_out=cell(I_YD)), deps=[a2], nf=True)
            a8 = v(lambda e, tt=tt: e.scalar_tensor_tensor(
                scr6[0:1, 6:8], self.bkr[0:1, 2 * tt:2 * tt + 2], 1.0,
                st[0:1, I_U:I_U + 2], ALU.mult, ALU.mult, accum_out=cell(I_G1B)),
                deps=pd("c5"), nf=True)
            a9 = v(lambda e: e.reduce_sum(cell(I_G12), st[0:1, I_YD:I_YD + 2],
                                          axis=AX.X), deps=[a7, a8])
            a10 = v(lambda e: e.scalar_tensor_tensor(
                scr[0:1, :], p_ch[0:1, 256:256 + C], 1.0, ghr[0:1, :],
                ALU.mult, ALU.mult, accum_out=cell(I_WSQ)), deps=[t2, a1], nf=True)
            a11 = v(lambda e: e.reduce_sum(cell(I_C2), st[0:1, I_AL:I_AL + 3],
                                           axis=AX.X), deps=[a5, a6, a10])
            a12 = v(lambda e: e.tensor_single_scalar(cell(I_C2), cell(I_C2), 0.0,
                                                     ALU.max))
            s1 = s(lambda e: e.activation(out=cell(I_CC), in_=cell(I_C2),
                                          func=AF.Sqrt), deps=[a12])
            v(lambda e, tt=tt: e.tensor_scalar(
                cell(I_T1), cell(I_CC), self.aamr[0:1, tt:tt + 1], cell(I_G12),
                ALU.mult, ALU.add), deps=[s1, a9])
            v(lambda e: e.tensor_scalar(cell(I_VAR), cell(I_T1), cell(I_CC),
                                        cell(I_NU), ALU.mult, ALU.add))
            v(lambda e: e.tensor_single_scalar(cell(I_SIG2), cell(I_VAR),
                                               EPS_LN, ALU.add))
            b4 = v(lambda e: e.reciprocal(cell(I_IS2), cell(I_SIG2)))
            s2 = s(lambda e: e.activation(out=cell(I_ISG), in_=cell(I_IS2),
                                          func=AF.Sqrt), deps=[b4])
            b5 = v(lambda e, tt=tt: e.tensor_scalar(
                cell(I_MU), cell(I_CC), self.mhatr[0:1, tt:tt + 1], None, ALU.mult),
                deps=[s1])
            c1 = v(lambda e: e.tensor_scalar(phr[0:1, :], phr[0:1, :], cell(I_ISG),
                                             None, ALU.mult), deps=[s2])
            c2 = v(lambda e, tt=tt: e.tensor_scalar(
                phr[0:1, tt:tt + 1], cell(I_CC), cell(I_ISG), None, ALU.mult),
                deps=[s2], nf=True)
            c3 = v(lambda e: e.tensor_scalar(cell(I_U), cell(I_U), cell(I_ISG),
                                             None, ALU.mult))
            v(lambda e: e.tensor_scalar(cell(I_E), cell(I_MU), -1.0, cell(I_E),
                                        ALU.mult, ALU.add), deps=[b5])
            c5 = v(lambda e: e.tensor_scalar(cell(I_E), cell(I_E), cell(I_ISG),
                                             None, ALU.mult))
            c6 = v(lambda e: e.tensor_scalar(cell(I_NU), cell(I_VAR), cell(I_IS2),
                                             None, ALU.mult), deps=[b4], nf=True)
            c7 = v(lambda e: e.tensor_scalar(st[0:1, I_U2:I_U2 + 2],
                                             st[0:1, I_U:I_U + 2], cell(I_U),
                                             None, ALU.mult))
            c8 = v(lambda e: e.tensor_scalar(cell(I_E2), cell(I_E), cell(I_E),
                                             None, ALU.mult))
            c9 = v(lambda e, tt=tt: e.tensor_scalar(
                cell(I_EK), cell(I_E), self.kapr[0:1, tt + 1:tt + 2], None,
                ALU.mult))
            last_ops = [c1, c2, c3, c5, c6, c7, c8, c9, s1, s2, a10, gce, t2]
            pv = {"c2": c2, "c3": c3, "c5": c5, "c8": c8}
        return last_ops

    def emit_mupdate(self, cc, last, vops):
        v, t = self.v, self.t
        C, PB = self.C, self.PB
        st = self.st
        I_U = 4
        vsl = lambda ec: self.v_pb[:, ec * PB + cc * C: ec * PB + cc * C + C]
        phb = t(lambda e: e.matmul(self.p_f[0:C, 128:128 + C], self.ones_r[0:1, 0:C],
                                   self.phr[0:1, 0:C], start=True, stop=True),
                deps=last + self.op_ones)
        kphv = v(lambda e, cc=cc: e.tensor_tensor(
            self.kph[:, 0:C], self.ke_pb[:, cc * C:cc * C + C],
            self.p_f[0:F, 128:128 + C], ALU.mult), deps=[phb])
        kpht = t(lambda e: e.transpose(self.p_tp[0:C, 0:F], self.kph[:, 0:C],
                                       self.ident[0:F, 0:F]),
                 deps=[kphv, self.op_identb])
        kphe = v(lambda e: e.tensor_copy(self.kphT[:, 0:F], self.p_tp[0:C, 0:F]),
                 deps=[kpht])
        vt_ops = []
        for dc in range(8):
            sl = (dc % 2) * 128
            war = vt_ops[-1:] if vt_ops else [kphe]
            tp = t(lambda e, dc=dc, sl=sl: e.transpose(
                self.p_tp[0:C, sl:sl + 128], vsl(dc)[:, :], self.ident[:, :]),
                deps=[vops[dc], self.op_identb] + war + last)
            ev = v(lambda e, dc=dc, sl=sl: e.tensor_copy(
                self.vtok[:, dc * 128:(dc + 1) * 128], self.p_tp[0:C, sl:sl + 128]),
                deps=[tp])
            vt_ops.append(ev)
        mads = []
        for half in range(2):
            mads.append(t(lambda e, half=half: e.matmul(
                (self.p_a if half == 0 else self.p_b)[0:F, 0:512],
                self.kphT[:, 0:F], self.vtok[:, half * 512:(half + 1) * 512],
                start=True, stop=True), deps=[kphe] + vt_ops + last))
        ueb = t(lambda e: e.matmul(self.p_c[0:F, 300:302], self.ones_r[0:1, 0:F],
                                   st[0:1, I_U:I_U + 2], start=True, stop=True),
                deps=last + self.op_ones)
        uee = v(lambda e: e.tensor_copy(self.uebc[:, 0:2], self.p_c[0:F, 300:302]),
                deps=[ueb])
        mnew = []
        for half in range(2):
            v(lambda e, half=half: e.scalar_tensor_tensor(
                self.m_prev[:, half * 512:(half + 1) * 512],
                self.m_prev[:, half * 512:(half + 1) * 512], self.uebc[:, 0:1],
                (self.p_a if half == 0 else self.p_b)[0:F, 0:512],
                ALU.mult, ALU.add), deps=[uee, mads[half]] + self.chain_mbf_dep)
            v(lambda e, half=half: e.tensor_scalar(
                self.m_prev[:, half * 512:(half + 1) * 512],
                self.m_prev[:, half * 512:(half + 1) * 512], self.uebc[:, 1:2],
                None, ALU.add))
            m3 = v(lambda e, half=half: e.tensor_copy(
                self.m_bf[:, half * 512:(half + 1) * 512],
                self.m_prev[:, half * 512:(half + 1) * 512]))
            mnew.append(m3)
        self.chain_mbf_dep = mnew
        return mnew + [kphe, uee] + last

    def p5_post(self):
        v, s, t, dma, dmat = self.v, self.s, self.t, self.dma, self.dmat
        T, PB, NPB = self.T, self.PB, self.NPB
        chain_tail = list(getattr(self, "chain_done", []))
        wo_done = self.prep_w(self.wv_sb, self.woT_h, D, False,
                              extra_deps=chain_tail)
        prev = {}
        self.s_sc_ops = []
        mo = self.v_pb
        sfm = self.xnr_pb
        q_pb, qb_pb = self.k_pb, self.kb_pb
        for pb in range(NPB):
            tiles = range(pb * (PB // 128), (pb + 1) * (PB // 128))
            sc_deps = [self.xr_ts[i][c] for i in tiles for c in range(8)]
            xnr_d = []
            for c in range(8):
                xnr_d.append(dma("xpq", self.xnr_pb[:, c * PB:(c + 1) * PB],
                                 self.xnr_sc[c, :, pb * PB:(pb + 1) * PB],
                                 deps=sc_deps + chain_tail + prev.get("xnr", [])))
            qmm = None
            for dc in range(8):
                qmm = t(lambda e, dc=dc: e.matmul(
                    self.p_c[0:F, 0:PB], self.wq_sb[:, dc * F:(dc + 1) * F],
                    self.xnr_pb[:, dc * PB:(dc + 1) * PB],
                    start=(dc == 0), stop=(dc == 7)),
                    deps=[xnr_d[dc]] + self.wq_done + chain_tail + prev.get("q", [])
                    )
            mn = v(lambda e: e.tensor_single_scalar(self.pscr[:, 0:PB],
                                                    self.p_c[0:F, 0:PB], 0.0,
                                                    ALU.min),
                   deps=[qmm] + chain_tail + prev.get("pscr", []))
            exq = s(lambda e: e.activation(out=self.pscr[:, 0:PB],
                                           in_=self.pscr[:, 0:PB], func=AF.Exp),
                    deps=[mn] + chain_tail)
            qf = v(lambda e: e.scalar_tensor_tensor(q_pb[:, 0:PB],
                                                    self.p_c[0:F, 0:PB], 0.0,
                                                    self.pscr[:, 0:PB],
                                                    ALU.max, ALU.add),
                   deps=[exq] + chain_tail)
            qb = v(lambda e: e.tensor_copy(qb_pb[:, 0:PB], q_pb[:, 0:PB]),
                   deps=chain_tail)
            prev["q"] = [mn, qf, qb]
            prev["pscr"] = [qf]
            moe = []
            for dc in range(8):
                pban = self.p_a if dc % 2 == 0 else self.p_b
                mm = t(lambda e, dc=dc, pban=pban: e.matmul(
                    pban[0:128, 0:PB], self.m_bf[:, dc * 128:(dc + 1) * 128],
                    qb_pb[:, 0:PB], start=True, stop=True),
                    deps=[qb] + chain_tail + self.chain_mbf_dep +
                    prev.get(("mo", dc % 2), []))
                ev = v(lambda e, dc=dc, pban=pban: e.tensor_copy(
                    mo[:, dc * PB:(dc + 1) * PB], pban[0:128, 0:PB]),
                    deps=[mm] + ([qmm] if dc < 2 else []))
                prev[("mo", dc % 2)] = [ev]
                moe.append(ev)
            sfm_ops = []
            for ec in range(8):
                pban = self.p_d if ec % 2 == 0 else self.p_e
                mm = None
                for dc in range(8):
                    mm = t(lambda e, ec=ec, dc=dc, pban=pban: e.matmul(
                        pban[0:128, 0:PB],
                        self.wv_sb[:, dc * D + ec * 128: dc * D + (ec + 1) * 128],
                        mo[:, dc * PB:(dc + 1) * PB],
                        start=(dc == 0), stop=(dc == 7)),
                        deps=[moe[dc]] + wo_done + chain_tail +
                        prev.get(("s", ec % 2), []))
                xnd = dma("sxn", self.xnsl[:, 0:PB],
                          self.xn_sc[ec, :, pb * PB:(pb + 1) * PB],
                          deps=[self.xn_ts[i][ec] for i in tiles] +
                          prev.get("sxn", []))
                ev = v(lambda e, ec=ec, pban=pban: e.tensor_tensor(
                    sfm[:, ec * PB:(ec + 1) * PB], self.xnsl[:, 0:PB],
                    pban[0:128, 0:PB], ALU.add),
                    deps=[mm, xnd] + ([qmm] if ec < 2 else []))
                prev[("s", ec % 2)] = [ev]
                prev["sxn"] = [ev]
                sfm_ops.append(ev)
            strp_prev = prev.get("strp", [])
            str_ops = []
            for j in range(PB // 128):
                trs = []
                for ec in range(8):
                    trs.append(dmat("str", self.stg[:, ec * 128:(ec + 1) * 128],
                                    sfm[:, ec * PB + j * 128:
                                        ec * PB + (j + 1) * 128],
                                    deps=[sfm_ops[ec]] + strp_prev))
                sto = dma("sts",
                          self.s_sc[pb * PB + j * 128: pb * PB + (j + 1) * 128, :],
                          self.stg[:, :], deps=trs)
                strp_prev = [sto]
                str_ops.append(sto)
                self.s_sc_ops.append(sto)
            prev["strp"] = strp_prev
            prev["xnr"] = [qmm] + moe + sfm_ops
        self.dbg_add("sfm", [128, 8 * PB], lambda: sfm[:, :],
                     deps=sfm_ops + str_ops, d=BF16)

    def p6_ffn(self):
        v, s, t, g, dma = self.v, self.s, self.t, self.g, self.dma
        T, QT = self.T, self.QT
        PIECE = self.PIECE
        prev = {}
        bufdeps = {0: [], 1: []}
        post_tail = (list(getattr(self, "s_sc_ops", []))
                     + getattr(self, "gate_done", []))
        out_ops = []
        yv = self.y_q[0:128, :].rearrange("p (j d) -> p j d", d=D)
        nxtbuf = 0
        for q in range(self.NQ):
            q0 = q * QT
            xnq = []
            tiles = range(q0 // 128, (q0 + QT) // 128)
            for c in range(8):
                xnq.append(dma("fxn", self.xnsl[:, c * QT:(c + 1) * QT],
                               self.xn_sc[c, :, q0:q0 + QT],
                               deps=[self.xn_ts[i][c] for i in tiles] +
                               prev.get("xnq", []) + post_tail))
            yi1 = g(lambda e: e.memset(self.y_q[:, :], 0.0),
                    deps=prev.get("y", []) + post_tail)
            yi2 = g(lambda e: e.memset(yv[:, :, 0:1], 1.0), deps=[yi1])
            ydep = [yi2]
            gsl = None
            last_xn_mm = []
            for e_ix in range(NE + 1):
                shared = (e_ix == NE)
                ncol1 = 8192 if shared else 4096
                ncol2 = 8 * 1023 if shared else 4 * 1023
                off3 = ncol1
                off2 = 2 * ncol1
                srcw = self.wsh_sc if shared else self.wffn_sc[e_ix]
                w1b = nxtbuf
                pc1 = dma("wl%d" % w1b, self.wff[:, w1b * PIECE: w1b * PIECE + ncol1],
                          srcw[:, 0:ncol1],
                          deps=(self.wffn_done if q == 0 else []) + bufdeps[w1b])
                w3b = 1 - w1b
                pc3 = dma("wl%d" % w3b, self.wff[:, w3b * PIECE: w3b * PIECE + ncol1],
                          srcw[:, off3:off3 + ncol1],
                          deps=(self.wffn_done if q == 0 else []) + bufdeps[w3b])
                if shared:
                    sksp = 1024
                else:
                    sksp = 512
                wsl1 = lambda c, m, w1b=w1b, sksp=sksp: self.wff[
                    :, w1b * PIECE + c * sksp + m * 128:
                    w1b * PIECE + c * sksp + (m + 1) * 128]
                wsl3 = lambda c, m, w3b=w3b, sksp=sksp: self.wff[
                    :, w3b * PIECE + c * sksp + m * 128:
                    w3b * PIECE + c * sksp + (m + 1) * 128]
                b1d = dma("bl", self.becol[:, 0:4],
                          self.bs1_h[:, 0:4] if shared else self.be1_h[e_ix, :, :],
                          deps=prev.get("bc", []))
                b3d = dma("bl", self.becol[:, 4:8],
                          self.bs3_h[:, 0:4] if shared else self.be3_h[e_ix, :, :],
                          deps=prev.get("bc", []))
                bx = []
                if shared:
                    bx = [dma("bl", self.bshx[:, 0:4], self.bs1_h[:, 4:8],
                              deps=prev.get("bc2", [])),
                          dma("bl", self.bshx[:, 4:8], self.bs3_h[:, 4:8],
                              deps=prev.get("bc2", []))]
                b2d = dma("bl", self.be2row[0:1, 0:1023],
                          self.bs2_h[0:1, :] if shared else self.be2_h[e_ix, 0:1, :],
                          deps=prev.get("bc", []))
                wtd = dma("bl", self.wtmp[0:1, 0:1023],
                          self.ws2ti_h[0:1, :] if shared else self.w2ti_h[e_ix, 0:1, :],
                          deps=prev.get("w2t", []))
                wtc = v(lambda e: e.tensor_copy(self.w2time[0:1, 0:1023],
                                                self.wtmp[0:1, 0:1023]),
                        deps=[wtd] + prev.get("w2tc", []))
                bdeps = [b1d, b3d, b2d, wtc] + bx
                NM = 8 if shared else 4
                NKH = 8 if shared else 4
                NFE = 1023
                l1ev, l3ev = {}, {}
                pb_l = [self.p_a, self.p_b, self.p_c, self.p_d]
                l1mm_last = {}
                for mpass in range(NM // 4):
                    for mi in range(4):
                        m = mpass * 4 + mi
                        mm = None
                        for c in range(8):
                            mm = t(lambda e, c=c, m=m, mi=mi, wsl1=wsl1: e.matmul(
                                pb_l[mi][0:128, 0:QT], wsl1(c, m),
                                self.xnsl[:, c * QT:(c + 1) * QT],
                                start=(c == 0), stop=(c == 7)),
                                deps=[pc1, xnq[c]] + prev.get(("l1", mi), [])
                                + post_tail)
                        l1mm_last[mi] = mm
                        sev = s(lambda e, m=m, mi=mi: e.activation(
                            out=self.ab_sb[:, mi * 512:mi * 512 + QT],
                            in_=pb_l[mi][0:128, 0:QT], func=AF.Silu,
                            bias=(self.becol[:, m:m + 1] if m < 4
                                  else self.bshx[:, m - 4:m - 3])),
                            deps=[mm] + bdeps)
                        l1ev[m] = sev
                    for mi in range(4):
                        m = mpass * 4 + mi
                        mm = None
                        for c in range(8):
                            mm = t(lambda e, c=c, m=m, mi=mi, wsl3=wsl3: e.matmul(
                                pb_l[mi][0:128, 0:QT], wsl3(c, m),
                                self.xnsl[:, c * QT:(c + 1) * QT],
                                start=(c == 0), stop=(c == 7)),
                                deps=[pc3, l1ev[m]])
                        last_xn_mm = [mm]
                        hev = v(lambda e, m=m, mi=mi: e.scalar_tensor_tensor(
                            self.h_sb[:, m * 512:m * 512 + QT],
                            pb_l[mi][0:128, 0:QT],
                            (self.becol[:, 4 + m:5 + m] if m < 4
                             else self.bshx[:, m:m + 1]),
                            self.ab_sb[:, mi * 512:mi * 512 + QT],
                            ALU.add, ALU.mult),
                            deps=[mm, l1ev[m]] + bdeps + prev.get("huse", []))
                        l3ev[m] = hev
                        prev[("l1", mi)] = [hev]
                pc2 = dma("wl%d" % w1b, self.wff[:, w1b * PIECE: w1b * PIECE + ncol2],
                          srcw[:, off2:off2 + ncol2],
                          deps=(self.wffn_done if q == 0 else []) +
                          list(l1mm_last.values()))
                w2b = w1b
                wsl2 = lambda c, n0, n1, w2b=w2b: self.wff[
                    :, w2b * PIECE + c * 1023 + n0: w2b * PIECE + c * 1023 + n1]
                nxtbuf = w3b
                bufdeps[w3b] = []
                sqp = []
                for m in range(NM):
                    sq = s(lambda e, m=m: e.activation(
                        out=self.sqf[:, 0:QT],
                        in_=self.h_sb[:, m * 512:m * 512 + QT],
                        func=AF.Square), deps=[l3ev[m]] + sqp[-1:])
                    sqm = t(lambda e, m=m, NM=NM: e.matmul(
                        self.p_e[0:1, 0:QT], self.ones_cf[:, 0:1],
                        self.sqf[:, 0:QT],
                        start=(m == 0), stop=(m == NM - 1)),
                        deps=[sq] + self.op_ones + prev.get("tsq", []) +
                        prev.get("l2e", []))
                    sqp.append(sqm)
                htw = s(lambda e: e.activation(out=self.ht_row[0:1, 0:QT],
                                               in_=self.p_e[0:1, 0:QT],
                                               func=AF.Sqrt, bias=1.0),
                        deps=[sqp[-1]])
                prev["tsq"] = [htw]
                for m in range(QT // 128):
                    jj = m
                    tile_ix = q0 // 128 + m
                    mmlo = mmhi = None
                    for c in range(NKH):
                        mmlo = t(lambda e, c=c, m=m, wsl2=wsl2: e.matmul(
                            self.p_e[0:128, 0:512],
                            self.h_sb[:, c * 512 + m * 128: c * 512 + (m + 1) * 128],
                            wsl2(c, 0, 512), start=(c == 0), stop=False),
                            deps=[l3ev[c], pc2] + prev.get("l2e", []) +
                            ([htw] if c == 0 else []))
                        mmhi = t(lambda e, c=c, m=m, wsl2=wsl2: e.matmul(
                            self.p_f[0:128, 0:NFE - 512],
                            self.h_sb[:, c * 512 + m * 128: c * 512 + (m + 1) * 128],
                            wsl2(c, 512, NFE), start=(c == 0), stop=False),
                            deps=[l3ev[c], pc2] + prev.get("l2e", []))
                    tlo = t(lambda e, m=m: e.matmul(
                        self.p_e[0:128, 0:512],
                        self.ht_row[0:1, m * 128:(m + 1) * 128],
                        self.w2time[0:1, 0:512], start=False, stop=False),
                        deps=[htw, wtc])
                    thi = t(lambda e, m=m: e.matmul(
                        self.p_f[0:128, 0:NFE - 512],
                        self.ht_row[0:1, m * 128:(m + 1) * 128],
                        self.w2time[0:1, 512:NFE], start=False, stop=False),
                        deps=[htw, wtc])
                    blo = t(lambda e, m=m: e.matmul(
                        self.p_e[0:128, 0:512],
                        self.ones_r[0:1, 0:128],
                        self.be2row[0:1, 0:512], start=False, stop=True),
                        deps=bdeps + self.op_ones)
                    bhi = t(lambda e, m=m: e.matmul(
                        self.p_f[0:128, 0:NFE - 512],
                        self.ones_r[0:1, 0:128],
                        self.be2row[0:1, 512:NFE], start=False, stop=True),
                        deps=bdeps + self.op_ones)
                    ysl = yv[:, jj, :]
                    yspc = self.y_q[:, jj * D + 1: jj * D + D]
                    ycol = self.y_q[:, jj * D: jj * D + 1]
                    wc = (1.0 if shared else
                          self.w16[:, tile_ix * NE + e_ix: tile_ix * NE + e_ix + 1])
                    s2e1 = v(lambda e: e.tensor_copy(self.s2_sb[:, 0:512],
                                                     self.p_e[0:128, 0:512]),
                             deps=[blo, tlo, mmlo])
                    s2e2 = v(lambda e: e.tensor_copy(self.s2_sb[:, 512:NFE],
                                                     self.p_f[0:128, 0:NFE - 512]),
                             deps=[bhi, thi, mmhi])
                    zt1 = s(lambda e: e.activation(
                        out=self.sqf[:, 0:512], in_=self.s2_sb[:, 0:512],
                        func=AF.Square, accum_out=self.colsc[:, 0:1]),
                        deps=[s2e1, htw], acc=True)
                    zt2 = s(lambda e: e.activation(
                        out=self.sqf[:, 0:NFE - 512], in_=self.s2_sb[:, 512:NFE],
                        func=AF.Square, accum_out=self.colsc[:, 1:2]),
                        deps=[s2e2, zt1], acc=True)
                    prev["l2e"] = [s2e1, s2e2, zt1, zt2]
                    t2s = v(lambda e: e.tensor_add(self.colsc[:, 2:3],
                                                   self.colsc[:, 0:1],
                                                   self.colsc[:, 1:2]),
                            deps=[zt1, zt2])
                    t2c = s(lambda e: e.activation(out=self.colsc[:, 3:4],
                                                   in_=self.colsc[:, 2:3],
                                                   func=AF.Sqrt, bias=1.0),
                            deps=[t2s])
                    z1 = v(lambda e, wc=wc, yspc=yspc: e.scalar_tensor_tensor(
                        yspc, self.s2_sb[:, 0:NFE], wc, yspc, ALU.mult, ALU.add),
                        deps=[s2e1, s2e2] + ydep +
                        (getattr(self, "gate_done", [])[:1] if not shared else []))
                    z2 = v(lambda e, wc=wc, ycol=ycol: e.scalar_tensor_tensor(
                        ycol, self.colsc[:, 3:4], wc, ycol, ALU.mult, ALU.add),
                        deps=[t2c] + ydep)
                    zq = s(lambda e, ysl=ysl: e.activation(
                        out=self.s2_sb[:, 0:1024], in_=ysl, func=AF.Square,
                        accum_out=self.colsc[:, 4:5]), deps=[z1, z2], acc=True)
                    v(lambda e, ycol=ycol: e.tensor_scalar(
                        self.colsc[:, 5:6], ycol, ycol, 2.0, ALU.mult, ALU.mult),
                        deps=[z1, z2])
                    v(lambda e: e.tensor_sub(self.colsc[:, 5:6], self.colsc[:, 5:6],
                                             self.colsc[:, 4:5]), deps=[zq])
                    v(lambda e: e.tensor_scalar_max(self.colsc[:, 5:6],
                                                    self.colsc[:, 5:6], 1e-6))
                    i2d = v(lambda e: e.reciprocal(self.colsc[:, 6:7],
                                                   self.colsc[:, 5:6]))
                    iscl = s(lambda e: e.activation(out=self.colsc[:, 7:8],
                                                    in_=self.colsc[:, 6:7],
                                                    func=AF.Sqrt), deps=[i2d])
                    ysc = s(lambda e, ysl=ysl: e.activation(
                        out=ysl, in_=ysl, func=AF.Identity, scale=self.colsc[:, 7:8]),
                        deps=[iscl, zq])
                    ydep = [ysc]
                    prev["huse"] = [mmlo, mmhi]
                bufdeps[w3b] = [l3ev[m] for m in range(NM)] + last_xn_mm
                bufdeps[w2b] = [s2e1, s2e2, mmlo, mmhi]
                prev["bc"] = [ysc, l1ev[NM - 1]]
                prev["bc2"] = [ysc, l1ev[NM - 1]]
                prev["w2t"] = [mmlo, mmhi]
                prev["w2tc"] = [mmlo, mmhi]
            for jj in range(QT // 128):
                row0 = q0 + jj * 128
                ysl = yv[:, jj, :]
                gsl = s(lambda e, ysl=ysl: e.activation(
                    out=self.xsq[:, :], in_=ysl, func=AF.Silu),
                    deps=ydep + prev.get("og", []))
                dx = dma("fx", self.xt[:, :], self.x_hbm[row0:row0 + 128, :],
                         deps=prev.get("fx", []))
                ds = dma("fs", self.stg[:, 0:1024],
                         self.s_sc[row0:row0 + 128, :],
                         deps=self.s_sc_ops + prev.get("fs", []))
                c1 = v(lambda e: e.tensor_tensor(self.xsq[:, :], self.stg[:, 0:1024],
                                                 self.xsq[:, :], ALU.mult),
                       deps=[gsl, ds])
                c2 = v(lambda e: e.tensor_tensor(self.xsq[:, :], self.xt[:, :],
                                                 self.xsq[:, :], ALU.add),
                       deps=[dx])
                do = dma("fo", self.out_hbm[row0:row0 + 128, :], self.xsq[:, :],
                         deps=[c2])
                prev["og"] = [c2, do]
                prev["fx"] = [c2]
                prev["fs"] = [c1]
                out_ops.append(do)
            prev["y"] = [gsl]
            prev["xnq"] = last_xn_mm + [gsl]
        self.last_dma_out += out_ops
        self.dbg_add("yq", [128, (QT // 128) * D], lambda: self.y_q[:, :],
                     deps=ydep)



NE = 16
D = 1024
F = 64


def _padz(a, n, axis):
    pad = [(0, 0)] * a.ndim
    pad[axis] = (0, n - a.shape[axis])
    return np.pad(a, pad)


def make_core_inputs(inp, b):
    f = np.float32
    We1, We3, We2 = inp["L0_ffn_We1"], inp["L0_ffn_We3"], inp["L0_ffn_We2"]
    Ws1, Ws3, Ws2 = inp["L0_ffn_Ws1"], inp["L0_ffn_Ws3"], inp["L0_ffn_Ws2"]
    be1, be3, be2 = inp["L0_ffn_be1"], inp["L0_ffn_be3"], inp["L0_ffn_be2"]
    bs1, bs3, bs2 = inp["L0_ffn_bs1"], inp["L0_ffn_bs3"], inp["L0_ffn_bs2"]
    d = {
        "x": np.ascontiguousarray(inp["L0_attn_x"][b], f),
        "wkT": np.ascontiguousarray(inp["L0_attn_Wk"].T, f),
        "wqT": np.ascontiguousarray(inp["L0_attn_Wq"].T, f),
        "wvT": np.ascontiguousarray(inp["L0_attn_Wv"].T, f),
        "woT": np.ascontiguousarray(inp["L0_attn_Wout"].T, f),
        "nw": np.ascontiguousarray(
            np.asarray(inp["L0_attn_norm_weight"], f).reshape(8, 128).T),
        "eta": np.ascontiguousarray(np.asarray(inp["L0_attn_eta"], f).reshape(F, 1)),
        "ident_in": np.eye(128, dtype=f),
        "gwT": np.ascontiguousarray(np.asarray(inp["L0_ffn_gate_w"], f).T),
        "we1T": np.ascontiguousarray(_padz(We1, 512, 1).transpose(0, 2, 1), f),
        "we3T": np.ascontiguousarray(_padz(We3, 512, 1).transpose(0, 2, 1), f),
        "we2T": np.ascontiguousarray(
            _padz(We2[:, :, 1:].transpose(0, 2, 1), 512, 1), f),
        "w2ti": np.ascontiguousarray(We2[:, :, 0].reshape(NE, 1, 1023), f),
        "be1": np.ascontiguousarray(
            _padz(be1, 512, 1).reshape(NE, 4, 128).transpose(0, 2, 1), f),
        "be3": np.ascontiguousarray(
            _padz(be3, 512, 1).reshape(NE, 4, 128).transpose(0, 2, 1), f),
        "be2": np.ascontiguousarray(be2.reshape(NE, 1, 1023), f),
        "ws1T": np.ascontiguousarray(_padz(Ws1, 1024, 0).T, f),
        "ws3T": np.ascontiguousarray(_padz(Ws3, 1024, 0).T, f),
        "ws2T": np.ascontiguousarray(_padz(Ws2[:, 1:].T, 1024, 0), f),
        "ws2ti": np.ascontiguousarray(Ws2[:, 0].reshape(1, 1023), f),
        "bs1": np.ascontiguousarray(_padz(bs1, 1024, 0).reshape(8, 128).T, f),
        "bs3": np.ascontiguousarray(_padz(bs3, 1024, 0).reshape(8, 128).T, f),
        "bs2": np.ascontiguousarray(bs2.reshape(1, 1023), f),
    }
    return d


def rand_inputs(T=4096, B=1, seed=0):
    rng = np.random.default_rng(seed)
    f = np.float32
    r = lambda *s: rng.standard_normal(s).astype(f)
    return {
        "L0_attn_x": r(B, T, D),
        "L0_attn_norm_weight": (r(D) * 0.5 + 1.0).astype(f),
        "L0_attn_Wk": r(F, D) * 0.02,
        "L0_attn_Wv": r(D, D) * 0.02,
        "L0_attn_Wq": r(F, D) * 0.02,
        "L0_attn_eta": r(F),
        "L0_attn_Wout": r(D, D) * 0.02,
        "L0_ffn_gate_w": r(NE, D) * 0.02,
        "L0_ffn_We1": r(NE, 511, D) * 0.02,
        "L0_ffn_be1": r(NE, 511) * 0.1,
        "L0_ffn_We3": r(NE, 511, D) * 0.02,
        "L0_ffn_be3": r(NE, 511) * 0.1,
        "L0_ffn_We2": r(NE, 1023, 512) * 0.02,
        "L0_ffn_be2": r(NE, 1023) * 0.1,
        "L0_ffn_Ws1": r(1023, D) * 0.02,
        "L0_ffn_bs1": r(1023) * 0.1,
        "L0_ffn_Ws3": r(1023, D) * 0.02,
        "L0_ffn_bs3": r(1023) * 0.1,
        "L0_ffn_Ws2": r(1023, 1024) * 0.02,
        "L0_ffn_bs2": r(1023) * 0.1,
    }


def kernel(**inputs):
    from concourse.bass_utils import run_bass_kernel_spmd
    T = 4096
    B = 8
    b = KB(T=T)
    nc = b.build()
    in_maps = [make_core_inputs(inputs, i) for i in range(B)]
    res = run_bass_kernel_spmd(nc, in_maps, core_ids=list(range(B)))
    out = np.stack([np.asarray(res.results[i]["out"], np.float32)
                    for i in range(B)], axis=0)
    return out
